# version 02
# baseline (speedup 1.0000x reference)
import numpy as np
import ml_dtypes
import concourse.bass as bass
import concourse.mybir as mybir
from concourse.bass_utils import run_bass_kernel_spmd

F32 = mybir.dt.float32
BF16 = mybir.dt.bfloat16
AF = mybir.ActivationFunctionType
ALU = mybir.AluOpType
AX = mybir.AxisListType

B, S, D = 4, 4096, 2048
HPC, DH = 8, 128
TPC = 2048
NE, DI = 32, 768
STW = 512
NST = S // STW
C_EXP = 8.0
EPS = 1e-5
QSCALE = 1.0 / float(np.sqrt(np.float32(128.0)))
NEG = -1e10

ENGS = ("pe", "dve", "act", "pool", "sp")
PHASES = ["PA", "PB", "PC", "PD", "PE", "F1", "F2", "PG"]


class Op:
    __slots__ = ("eng", "fn", "reads", "writes", "kind", "idx", "eseq",
                 "waits", "needs_inc", "inc_sem", "inc_amt", "tag")

    def __init__(self, eng, fn, reads, writes, kind, idx, eseq, tag):
        self.eng, self.fn = eng, fn
        self.reads, self.writes = reads, writes
        self.kind = kind
        self.idx, self.eseq = idx, eseq
        self.waits = []
        self.needs_inc = False
        self.inc_sem = None
        self.inc_amt = 0
        self.tag = tag


class Prog:

    def __init__(self):
        self.ops = []
        self.streams = {e: [] for e in ENGS}
        self.regs = {}
        self.clock = {e: {} for e in ENGS}
        self.last_comp = {e: None for e in ENGS}
        self.dma_count = {"sp": 0, "pool": 0}
        self.cc_count = 0

    def region(self, r):
        st = self.regs.get(r)
        if st is None:
            st = [None, []]
            self.regs[r] = st
        return st

    def new_region(self, r, preds=()):
        self.regs[r] = [None, [p for p in preds if p is not None]]

    def _edge(self, op, prod):
        if prod is op:
            return
        e, f = op.eng, prod.eng
        if e == "pe" and f == "pe":
            return
        if self.clock[e].get(f, -1) >= prod.eseq:
            return
        if prod.kind in ("d", "cc"):
            op.waits.append((prod.inc_sem, prod.inc_amt))
            self.clock[e][f] = prod.eseq
            return
        if e == f:
            sub = self.last_comp[e]
            assert sub is not None and sub.eseq >= prod.eseq
            sub.needs_inc = True
            op.waits.append(sub)
            self.clock[e][f] = sub.eseq
        else:
            prod.needs_inc = True
            op.waits.append(prod)
            self.clock[e][f] = prod.eseq

    def emit(self, eng, fn, reads=(), writes=(), kind="c", tag=""):
        idx = len(self.ops)
        op = Op(eng, fn, list(reads), list(writes), kind, idx,
                len(self.streams[eng]), tag)
        self.ops.append(op)
        self.streams[eng].append(op)
        if kind == "d":
            self.dma_count[eng] += 1
            op.inc_sem = "sd" if eng == "sp" else "pd"
            op.inc_amt = 16 * self.dma_count[eng]
            op.needs_inc = True
        elif kind == "cc":
            self.cc_count += 1
            op.inc_sem = "cc"
            op.inc_amt = self.cc_count
            op.needs_inc = True
        prods = {}
        for r in op.reads:
            st = self.region(r)
            if st[0] is not None:
                prods[st[0].idx] = st[0]
        for r in op.writes:
            st = self.region(r)
            if st[0] is not None:
                prods[st[0].idx] = st[0]
            for rd in st[1]:
                prods[rd.idx] = rd
        for pidx in sorted(prods):
            self._edge(op, prods[pidx])
        for r in op.reads:
            self.region(r)[1].append(op)
        for r in op.writes:
            st = self.region(r)
            st[0] = op
            st[1] = []
        if kind == "c":
            self.last_comp[eng] = op
        return op

    def finalize(self):
        inc_no = {}
        totals = {}
        for e in ENGS:
            n = 0
            for op in self.streams[e]:
                if op.kind == "c" and op.needs_inc:
                    n += 1
                    op.inc_sem = e
                    inc_no[op.idx] = n
            totals[e] = n
        for op in self.ops:
            byname = {}
            for w in op.waits:
                if isinstance(w, tuple):
                    s, v = w
                else:
                    s, v = w.eng, inc_no[w.idx]
                if byname.get(s, 0) < v:
                    byname[s] = v
            op.waits = sorted(byname.items())
        totals["sd"] = 16 * self.dma_count["sp"]
        totals["pd"] = 16 * self.dma_count["pool"]
        totals["cc"] = self.cc_count
        for op in self.ops:
            for s, v in op.waits:
                assert v <= totals.get(s, 0), f"wait {s}>={v} exceeds total {totals.get(s)} ({op.tag})"
        return totals


class Arena:
    def __init__(self, prog, size):
        self.P = prog
        self.size = size
        self.cur = 0
        self.live = []
        self.marks = []
        self.freed = []

    def alloc(self, name, n):
        a, b = self.cur, self.cur + n
        assert b <= self.size, f"arena overflow: {name}: {b} > {self.size}"
        self.cur = b
        self.live.append((name, a, b))
        preds = []
        st_old = self.P.regs.get(name)
        if st_old is not None:
            preds += ([st_old[0]] if st_old[0] else []) + st_old[1]
        for (nm, fa, fb, fp) in self.freed:
            if fa < b and a < fb:
                preds += fp
        self.P.new_region(name, preds)
        return (a, b)

    def mark(self):
        self.marks.append((self.cur, list(self.live)))

    def release(self):
        cur0, live0 = self.marks.pop()
        for (nm, a, b) in self.live:
            if (nm, a, b) not in live0:
                st = self.P.regs.get(nm, [None, []])
                preds = ([st[0]] if st[0] else []) + st[1]
                self.freed.append((nm, a, b, preds))
        self.cur = cur0
        self.live = live0


def _host_consts():
    ident = np.eye(128, dtype=ml_dtypes.bfloat16)
    mask = np.zeros((128, 4 * 512), dtype=np.float32)
    for j in range(4):
        i = np.arange(128)[:, None]
        kk = np.arange(512)[None, :]
        mask[:, j * 512:(j + 1) * 512] = (kk <= 128 * j + i).astype(np.float32)
    return ident, mask.astype(ml_dtypes.bfloat16)


def build_program(upto="PG", taps=()):
    from contextlib import ExitStack
    PH = PHASES.index(upto)
    taps = set(taps)
    nc = bass.Bass("TRN2", target_bir_lowering=False)
    P = Prog()

    x_in = nc.dram_tensor("x", [S, D], F32, kind="ExternalInput").ap()
    xown_in = nc.dram_tensor("xown", [TPC, D], F32, kind="ExternalInput").ap()
    wqkv_in = nc.dram_tensor("wqkv", [D, 3072], F32, kind="ExternalInput").ap()
    wout_in = nc.dram_tensor("wout", [D, D], F32, kind="ExternalInput").ap()
    gate_in = nc.dram_tensor("gate", [D, NE], F32, kind="ExternalInput").ap()
    gup_in = nc.dram_tensor("gup", [NE, D, 2 * DI], F32, kind="ExternalInput").ap()
    gdn_in = nc.dram_tensor("gdn", [NE, DI, D], F32, kind="ExternalInput").ap()
    ident_in = nc.dram_tensor("ident", [128, 128], BF16, kind="ExternalInput").ap()
    maskc_in = nc.dram_tensor("maskc", [128, 2048], BF16, kind="ExternalInput").ap()
    out_ext = nc.dram_tensor("out", [TPC, D], F32, kind="ExternalOutput").ap()

    tap_ext = {}
    tap_shapes = {
        "qT": ([HPC, 128, S], BF16), "kT": ([HPC, 128, S], BF16),
        "v": ([S, HPC * DH], BF16), "ctx": ([HPC * 128, S], BF16),
        "y": ([TPC, D], F32), "rw": ([TPC, NE], F32),
        "gated": ([2, DI, TPC], BF16), "acc": ([TPC, D], F32),
        "ctx2": ([16 * 128, TPC], BF16),
    }
    for t in taps:
        tap_ext[t] = nc.dram_tensor("tap_" + t, *tap_shapes[t], kind="ExternalOutput").ap()

    xbf_h = nc.dram_tensor("xbf_h", [S, D], BF16).ap()
    qT_h = nc.dram_tensor("qT_h", [HPC, 128, S], BF16).ap()
    kT_h = nc.dram_tensor("kT_h", [HPC, 128, S], BF16).ap()
    v_h = nc.dram_tensor("v_h", [S, HPC * DH], BF16).ap()
    y_h = nc.dram_tensor("y_h", [TPC, D], F32).ap()
    ybf_h = nc.dram_tensor("ybf_h", [TPC, D], BF16).ap()
    gated_h = nc.dram_tensor("gated_h", [NE, DI, TPC], BF16).ap()
    ccin = nc.dram_tensor("ccin", [HPC * 128, S // 2], F32)
    ccout = nc.dram_tensor("ccout", [2 * HPC * 128, S // 2], F32)

    with ExitStack() as ctx:
        E = ctx.enter_context
        AR = 94208
        arena = E(nc.sbuf_tensor("arena", [128, AR], BF16))
        ident = E(nc.sbuf_tensor("identt", [128, 128], BF16))
        maskc = E(nc.sbuf_tensor("maskct", [128, 2048], BF16))
        RW = E(nc.sbuf_tensor("RWt", [128, 512], F32))
        stats = E(nc.sbuf_tensor("statst", [128, 16], F32))
        sumblk = E(nc.sbuf_tensor("sumblkt", [128, 48], F32))
        rt = E(nc.sbuf_tensor("rtt", [128, 160], F32))
        stage = E(nc.sbuf_tensor("staget", [128, 4 * 512], BF16))
        sqscr = E(nc.sbuf_tensor("sqscrt", [128, 2048], F32))
        cbias = E(nc.sbuf_tensor("cbiast", [128, 2], F32))
        psum = E(nc.psum_tensor("psumt", [128, 4096], F32))

        sems = {}
        for snm in ("pe", "dve", "act", "pool", "sd", "pd", "cc"):
            sems[snm] = E(nc.semaphore("sem_" + snm))
        r_po = E(nc.gpsimd.register("r_po"))
        r_po2 = E(nc.gpsimd.register("r_po2"))
        block = E(nc.Block())

        A = Arena(P, AR)

        def ps(i, n=1):
            return psum.ap()[:, i * 512:(i + n) * 512]

        def psb4(i):
            return psum.ap()[:, i * 512:(i + 1) * 512].bitcast(BF16)[:, 0:512]

        PSR = [f"ps{i}" for i in range(8)]

        def bf(rng):
            return arena.ap()[:, rng[0]:rng[1]]

        def f32(rng):
            return arena.ap()[:, rng[0]:rng[1]].bitcast(F32)

        def ln_chain(src_ap, src_reg, dst_ap, dst_reg, par=0):
            q = par * 8
            lr = f"lnstats{par}"
            P.emit("dve", lambda o: o.tensor_scalar_mul(stats[:, q + 1:q + 2], stats[:, q:q + 1], -1.0 / 2048.0),
                   reads=[lr], writes=[lr], tag="lnm")
            P.emit("dve", lambda o: o.tensor_scalar_add(src_ap, src_ap, stats[:, q + 1:q + 2]),
                   reads=[lr, src_reg], writes=[src_reg], tag="lnt")
            P.emit("act", lambda o: o.activation(sqscr[:, :], src_ap, AF.Square,
                                                 accum_out=stats[:, q + 2:q + 3]),
                   reads=[src_reg], writes=[f"sqjunk{par}", lr], tag="lnsq")
            P.emit("act", lambda o: o.activation(stats[:, q + 3:q + 4], stats[:, q + 2:q + 3], AF.Ln,
                                                 scale=1.0 / 2048.0, bias=cbias[:, 1:2]),
                   reads=[lr, "cbias"], writes=[lr], tag="lnln")
            P.emit("act", lambda o: o.activation(stats[:, q + 4:q + 5], stats[:, q + 3:q + 4], AF.Exp, scale=-0.5),
                   reads=[lr], writes=[lr], tag="lnexp")
            P.emit("dve", lambda o: o.tensor_scalar_mul(dst_ap, src_ap, stats[:, q + 4:q + 5]),
                   reads=[lr, src_reg], writes=[dst_reg], tag="lny")

        P.emit("dve", lambda o: o.memset(cbias[:, 0:1], -C_EXP),
               writes=["cbias"], tag="cb0")
        P.emit("dve", lambda o: o.memset(cbias[:, 1:2], EPS),
               writes=["cbias"], tag="cb1")
        P.emit("sp", lambda o: o.dma_start(ident[:, :], ident_in),
               writes=["ident"], kind="d", tag="ld_ident")
        P.emit("sp", lambda o: o.dma_start(maskc[:, :], maskc_in),
               writes=["maskc"], kind="d", tag="ld_mask")
        def xcast(i):
            sl = slice(i * 512, (i + 1) * 512)
            P.emit("pool", lambda o, sl=sl: o.dma_start(xbf_h[sl, :], x_in[sl, :]),
                   writes=[f"d_xbf{i}"], kind="d", tag="xcast")
        xcast(0)

        A.mark()
        wq = A.alloc("wq", 16 * 1024)
        wk = A.alloc("wk", 16 * 1024)
        wv = A.alloc("wv", 16 * 1024)
        xT = [A.alloc(f"xT{r}", 16 * 512) for r in range(2)]
        for part, reg, rname in ((0, wq, "wq"), (1, wk, "wk"), (2, wv, "wv")):
            dst = bf(reg).rearrange("p (c n) -> p c n", n=1024)
            src = wqkv_in.rearrange("(c p) n -> p c n", p=128)[:, :, part * 1024:(part + 1) * 1024]
            for cc4 in range(4):
                csl = slice(cc4 * 4, cc4 * 4 + 4)
                P.emit("pool", lambda o, d=dst[:, csl, :], s=src[:, csl, :]: o.dma_start(d, s),
                       writes=[rname], kind="d", tag="ld_wqkv")
        for i in range(1, 8):
            xcast(i)

        def xT_ap(r, d, tsl=slice(0, 512)):
            a = xT[r][0]
            return arena.ap()[:, a + d * 512:a + (d + 1) * 512][:, tsl]

        stg_n = [0]

        def stage_slot():
            i = stg_n[0] % 4
            stg_n[0] += 1
            return stage.ap()[:, i * 512:(i + 1) * 512], f"stg{i}"

        if PH >= 0:
            for sti in range(NST):
                r = sti % 2
                t0 = sti * STW
                for d in range(16):
                    P.emit("sp", lambda o, d=d, r=r, t0=t0: o.dma_start(
                        xT_ap(r, d), xbf_h[t0:t0 + STW, d * 128:(d + 1) * 128], transpose=True),
                        reads=[f"d_xbf{sti}"], writes=[f"xT{r}"], kind="d", tag="xT")
                for wname, wreg, dest, dreg, scale in (
                        ("wk", wk, kT_h, "d_k", 1.0), ("wq", wq, qT_h, "d_q", QSCALE)):
                    for h in range(HPC):
                        bank = h % 4
                        for d in range(16):
                            wap = bf(wreg)[:, d * 1024 + h * 128: d * 1024 + (h + 1) * 128]
                            P.emit("pe", lambda o, w=wap, r=r, d=d, bank=bank, s=(d == 0), e=(d == 15):
                                   o.matmul(ps(bank), w, xT_ap(r, d), start=s, stop=e),
                                   reads=[wname, f"xT{r}"], writes=[PSR[bank]], tag="kqproj")
                        sap, sreg = stage_slot()
                        P.emit("act", lambda o, sap=sap, bank=bank, sc=scale:
                               o.activation(sap, ps(bank), AF.Copy, scale=sc),
                               reads=[PSR[bank]], writes=[sreg], tag="kqevac")
                        P.emit("sp", lambda o, sap=sap, h=h, t0=t0, dst=dest:
                               o.dma_start(dst[h, :, t0:t0 + STW], sap),
                               reads=[sreg], writes=[dreg], kind="d", tag="kqst")
                for sub in range(4):
                    for vch in range(2):
                        bank = 4 + ((sub * 2 + vch) % 4)
                        tsl = slice(sub * 128, (sub + 1) * 128)
                        for d in range(16):
                            P.emit("pe", lambda o, r=r, d=d, tsl=tsl, vch=vch, bank=bank, s=(d == 0), e=(d == 15):
                                   o.matmul(ps(bank), xT_ap(r, d, tsl),
                                            bf(wv)[:, d * 1024 + vch * 512: d * 1024 + (vch + 1) * 512],
                                            start=s, stop=e),
                                   reads=["wv", f"xT{r}"], writes=[PSR[bank]], tag="vproj")
                        sap, sreg = stage_slot()
                        P.emit("act", lambda o, sap=sap, bank=bank:
                               o.activation(sap, ps(bank), AF.Copy),
                               reads=[PSR[bank]], writes=[sreg], tag="vevac")
                        P.emit("sp", lambda o, sap=sap, t0=t0, sub=sub, vch=vch:
                               o.dma_start(v_h[t0 + sub * 128: t0 + (sub + 1) * 128,
                                               vch * 512:(vch + 1) * 512], sap),
                               reads=[sreg], writes=["d_v"], kind="d", tag="vst")
        A.release()

        for t in ("qT", "kT", "v"):
            if t in taps:
                src = {"qT": qT_h, "kT": kT_h, "v": v_h}[t]
                dreg = {"qT": "d_q", "kT": "d_k", "v": "d_v"}[t]
                P.emit("sp", lambda o, s=src, d2=tap_ext[t]: o.dma_start(d2, s),
                       reads=[dreg], kind="d", tag="tap")

        A.mark()
        ctx_r = A.alloc("ctx", HPC * 4096)
        A.mark()
        kb_ = [A.alloc(f"kb{r}", 4096) for r in range(2)]
        vb_ = [A.alloc(f"vb{r}", 4096) for r in range(2)]
        qb_ = [A.alloc(f"qb{r}", 4096) for r in range(2)]
        Pb_ = [A.alloc(f"P{r}", 4096) for r in range(2)]
        PT_r = A.alloc("PT", 32 * 512)

        if PH >= 1:
            for h in range(HPC):
                r = h % 2
                P.emit("sp", lambda o, h=h, r=r: o.dma_start(bf(kb_[r]), kT_h[h]),
                       reads=["d_k"], writes=[f"kb{r}"], kind="d", tag="kld")
                P.emit("sp", lambda o, h=h, r=r: o.dma_start(bf(qb_[r]), qT_h[h]),
                       reads=["d_q"], writes=[f"qb{r}"], kind="d", tag="qld")
                vsrc = v_h[:, h * 128:(h + 1) * 128].rearrange("(t p) n -> p t n", p=128)
                vdst = bf(vb_[r]).rearrange("p (t n) -> p t n", n=128)
                for q4 in range(4):
                    tsl = slice(q4 * 8, q4 * 8 + 8)
                    P.emit("sp", lambda o, s=vsrc[:, tsl, :], d2=vdst[:, tsl, :]: o.dma_start(d2, s),
                           reads=["d_v"], writes=[f"vb{r}"], kind="d", tag="vld")

                for s_ in range(NST):
                    NB = s_ + 1
                    Q0 = s_ * STW
                    cxb = 4 + (s_ % 2)

                    def pb_scores(j):
                        jp = j % 2
                        nslot = [0]
                        for kb in range(NB):
                            bank = [0, 6][(kb // 2) % 2] + (kb % 2)
                            P.emit("pe", lambda o, j=j, kb=kb, bank=bank, r=r, Q0=Q0:
                                   o.matmul(ps(bank),
                                            bf(qb_[r])[:, Q0 + j * 128: Q0 + (j + 1) * 128],
                                            bf(kb_[r])[:, kb * 512:(kb + 1) * 512],
                                            start=True, stop=True),
                                   reads=[f"qb{r}", f"kb{r}"], writes=[PSR[bank]], tag="sc")
                            diag = (kb == NB - 1)
                            if not diag and (kb % 2 == 1):
                                b0 = [0, 6][((kb - 1) // 2) % 2]
                                sl = nslot[0]; nslot[0] += 1
                                P.emit("act", lambda o, b0=b0, jp=jp, kb=kb, j=j, sl=sl:
                                       o.activation(bf(Pb_[jp])[:, (kb - 1) * 512:(kb + 1) * 512],
                                                    ps(b0, 2), AF.Exp, bias=cbias[:, 0:1],
                                                    accum_out=sumblk[:, j * 8 + sl: j * 8 + sl + 1]),
                                       reads=[PSR[b0], PSR[b0 + 1], "cbias"],
                                       writes=[f"P{jp}", "sumblk"], tag="exp2")
                            elif not diag and kb == NB - 2:
                                sl = nslot[0]; nslot[0] += 1
                                P.emit("act", lambda o, bank=bank, jp=jp, kb=kb, j=j, sl=sl:
                                       o.activation(bf(Pb_[jp])[:, kb * 512:(kb + 1) * 512],
                                                    ps(bank), AF.Exp, bias=cbias[:, 0:1],
                                                    accum_out=sumblk[:, j * 8 + sl: j * 8 + sl + 1]),
                                       reads=[PSR[bank], "cbias"],
                                       writes=[f"P{jp}", "sumblk"], tag="exp1")
                            elif diag:
                                P.emit("act", lambda o, bank=bank, jp=jp, kb=kb:
                                       o.activation(bf(Pb_[jp])[:, kb * 512:(kb + 1) * 512],
                                                    ps(bank), AF.Exp, bias=cbias[:, 0:1]),
                                       reads=[PSR[bank], "cbias"], writes=[f"P{jp}"], tag="expd")
                                sl = nslot[0]; nslot[0] += 1
                                P.emit("dve", lambda o, jp=jp, kb=kb, j=j, sl=sl:
                                       o.scalar_tensor_tensor(
                                           bf(Pb_[jp])[:, kb * 512:(kb + 1) * 512],
                                           bf(Pb_[jp])[:, kb * 512:(kb + 1) * 512], 1.0,
                                           maskc[:, j * 512:(j + 1) * 512],
                                           ALU.mult, ALU.mult,
                                           accum_out=sumblk[:, j * 8 + sl: j * 8 + sl + 1]),
                                       reads=[f"P{jp}", "maskc"],
                                       writes=[f"P{jp}", "sumblk"], tag="maskd")
                        return nslot[0]

                    def pb_post(j, nslots):
                        jp = j % 2
                        P.emit("dve", lambda o, j=j, n=nslots:
                               o.reduce_sum(rt[:, 128 + j: 129 + j], sumblk[:, j * 8: j * 8 + n], axis=AX.X),
                               reads=["sumblk"], writes=["rt"], tag="psum")
                        P.emit("dve", lambda o, j=j:
                               o.reciprocal(rt[:, 132 + j: 133 + j], rt[:, 128 + j: 129 + j]),
                               reads=["rt"], writes=["rt"], tag="prec")
                        P.emit("dve", lambda o, j=j, jp=jp, NB=NB:
                               o.tensor_scalar_mul(bf(Pb_[jp])[:, 0:NB * 512],
                                                   bf(Pb_[jp])[:, 0:NB * 512],
                                                   rt[:, 132 + j: 133 + j]),
                               reads=["rt", f"P{jp}"], writes=[f"P{jp}"], tag="pscale")

                    def pb_tr(j):
                        jp = j % 2
                        for kbq in range(NB):
                            bank = 2 + (kbq % 2)
                            for ii in range(4):
                                kb = kbq * 4 + ii
                                P.emit("pe", lambda o, jp=jp, kb=kb, bank=bank, ii=ii:
                                       o.transpose(psb4(bank)[:, ii * 128:(ii + 1) * 128],
                                                   bf(Pb_[jp])[:, kb * 128:(kb + 1) * 128],
                                                   ident[:, :]),
                                       reads=[f"P{jp}", "ident"], writes=[PSR[bank]], tag="tr")
                            dst = bf(PT_r).rearrange("p (k n) -> p k n", n=512)
                            P.emit("dve", lambda o, j=j, kbq=kbq, bank=bank, dst=dst:
                                   o.tensor_copy(
                                       dst[:, kbq * 4:(kbq + 1) * 4, j * 128:(j + 1) * 128],
                                       psb4(bank).rearrange("p (k n) -> p k n", n=128)),
                                   reads=[PSR[bank]], writes=["PT"], tag="trev")

                    def pb_av():
                        for kb in range(4 * NB):
                            P.emit("pe", lambda o, kb=kb, r=r, s4=(kb == 0), e4=(kb == 4 * NB - 1), bank=cxb:
                                   o.matmul(ps(bank),
                                            bf(vb_[r])[:, kb * 128:(kb + 1) * 128],
                                            bf(PT_r)[:, kb * 512:(kb + 1) * 512],
                                            start=s4, stop=e4),
                                   reads=[f"vb{r}", "PT"], writes=[PSR[cxb]], tag="av")
                        P.emit("dve", lambda o, h=h, Q0=Q0, bank=cxb:
                               o.tensor_copy(bf(ctx_r)[:, h * 4096 + Q0: h * 4096 + Q0 + STW],
                                             ps(bank)),
                               reads=[PSR[cxb]], writes=[f"ctx{h}"], tag="ctxev")

                    ns0 = pb_scores(0)
                    ns1 = pb_scores(1)
                    pb_post(0, ns0)
                    pb_tr(0)
                    ns2 = pb_scores(2)
                    pb_post(1, ns1)
                    pb_tr(1)
                    ns3 = pb_scores(3)
                    pb_post(2, ns2)
                    pb_tr(2)
                    pb_post(3, ns3)
                    pb_tr(3)
                    pb_av()
        A.release()

        if "ctx" in taps:
            for h in range(HPC):
                P.emit("sp", lambda o, h=h: o.dma_start(
                    tap_ext["ctx"][h * 128:(h + 1) * 128, :], bf(ctx_r)[:, h * 4096:(h + 1) * 4096]),
                    reads=[f"ctx{h}"], kind="d", tag="tapctx")

        if PH >= 2:
            for h in range(HPC):
                P.emit("sp", lambda o, h=h: o.dma_start(
                    ccin.ap()[h * 128:(h + 1) * 128, :],
                    bf(ctx_r)[:, h * 4096:(h + 1) * 4096].bitcast(F32)),
                    reads=[f"ctx{h}"], writes=["d_ccin"], kind="d", tag="ccst")
            P.emit("pool", lambda o: o.collective_compute(
                "AllGather", ALU.bypass,
                replica_groups=[[0, 1], [2, 3], [4, 5], [6, 7]],
                ins=[ccin.ap().opt()], outs=[ccout.ap().opt()]),
                reads=["d_ccin"], writes=["d_ccout"], kind="cc", tag="ccgather")

            def regsetup(o):
                pid = o.partition_id()
                o.reg_mov(r_po, pid)
                o.reg_mod(r_po, r_po, 2)
                o.reg_mul(r_po, r_po, 1024)
                o.reg_mov(r_po2, 1024 * 2048)
                return o.reg_add(r_po2, r_po2, r_po)
            P.emit("pool", regsetup, tag="regs")
        A.release()

        A.mark()
        ctx2_r = A.alloc("ctx2", 16 * 2048)
        if PH >= 2:
            ctx2f = bf(ctx2_r).bitcast(F32).rearrange("p (c n) -> p c n", n=1024)
            for half, roff in ((0, r_po), (1, r_po2)):
                src = bass.AP(ccout, roff, [[2048, 128], [128 * 2048, 8], [1, 1024]])
                P.emit("pool", lambda o, d=ctx2f[:, half * 8:(half + 1) * 8, :], s=src:
                       o.dma_start(d, s),
                       reads=["d_ccout"], writes=["ctx2"], kind="d", tag="ccld")
        if "ctx2" in taps:
            P.emit("sp", lambda o: o.dma_start(
                tap_ext["ctx2"].rearrange("(c p) n -> p c n", p=128),
                bf(ctx2_r).rearrange("p (c n) -> p c n", n=2048)),
                reads=["ctx2"], kind="d", tag="tapctx2")

        if PH >= 3:
            wout_r = A.alloc("wout", 16 * 2048)
            xs_ = [A.alloc(f"xs{r}", 2 * 2048) for r in range(2)]
            h1_ = [A.alloc(f"h1{r}", 2 * 2048) for r in range(2)]
            y_ = [A.alloc(f"yb{r}", 2 * 2048) for r in range(2)]
            dstw = bf(wout_r).rearrange("p (c n) -> p c n", n=2048)
            srcw = wout_in.rearrange("(c p) n -> p c n", p=128)
            for cc4 in range(4):
                csl = slice(cc4 * 4, cc4 * 4 + 4)
                P.emit("pool", lambda o, d=dstw[:, csl, :], s=srcw[:, csl, :]: o.dma_start(d, s),
                       writes=["wout"], kind="d", tag="ld_wout")
            for tt in range(16):
                r = tt % 2
                grp = 0 if r == 0 else 4
                banks = [PSR[grp + i] for i in range(4)]
                P.emit("sp", lambda o, tt=tt, r=r: o.dma_start(
                    f32(xs_[r]), xown_in[tt * 128:(tt + 1) * 128, :]),
                    writes=[f"xs{r}"], kind="d", tag="xld")
                for d in range(16):
                    for ct in range(4):
                        P.emit("pe", lambda o, d=d, ct=ct, tt=tt, grp=grp, s=(d == 0), e=(d == 15):
                               o.matmul(ps(grp + ct),
                                        bf(ctx2_r)[:, d * 2048 + tt * 128: d * 2048 + (tt + 1) * 128],
                                        bf(wout_r)[:, d * 2048 + ct * 512: d * 2048 + (ct + 1) * 512],
                                        start=s, stop=e),
                               reads=["ctx2", "wout"], writes=[banks[ct]], tag="oproj")
                P.emit("dve", lambda o, r=r, grp=grp:
                       o.scalar_tensor_tensor(f32(h1_[r]), ps(grp, 4), 1.0, f32(xs_[r]),
                                              ALU.mult, ALU.add, accum_out=stats[:, r * 8:r * 8 + 1]),
                       reads=banks + [f"xs{r}"], writes=[f"h1{r}", f"lnstats{r}"], tag="h1")
                ln_chain(f32(h1_[r]), f"h1{r}", f32(y_[r]), f"yb{r}", par=r)
                P.emit("sp", lambda o, tt=tt, r=r: o.dma_start(
                    y_h[tt * 128:(tt + 1) * 128, :], f32(y_[r])),
                    reads=[f"yb{r}"], writes=["d_y"], kind="d", tag="yst")
                P.emit("pool", lambda o, tt=tt, r=r: o.dma_start(
                    ybf_h[tt * 128:(tt + 1) * 128, :], f32(y_[r])),
                    reads=[f"yb{r}"], writes=["d_ybf"], kind="d", tag="ybfst")
        A.release()

        if "y" in taps:
            P.emit("sp", lambda o: o.dma_start(tap_ext["y"], y_h),
                   reads=["d_y"], kind="d", tag="tapy")

        A.mark()
        yT_r = A.alloc("yT", 16 * 2048)
        if PH >= 4:
            gateb_r = A.alloc("gateb", 16 * 32)
            for d in range(16):
                P.emit("sp", lambda o, d=d: o.dma_start(
                    bf(yT_r)[:, d * 2048:(d + 1) * 2048],
                    ybf_h[:, d * 128:(d + 1) * 128], transpose=True),
                    reads=["d_ybf"], writes=["yT"], kind="d", tag="yT")
            P.emit("pool", lambda o: o.dma_start(
                bf(gateb_r).rearrange("p (c n) -> p c n", n=32),
                gate_in.rearrange("(c p) n -> p c n", p=128)),
                writes=["gateb"], kind="d", tag="ld_gate")
            for tt in range(16):
                bank = 6 + tt % 2
                for d in range(16):
                    P.emit("pe", lambda o, d=d, tt=tt, bank=bank, s=(d == 0), e=(d == 15):
                           o.matmul(ps(bank)[:, 0:32],
                                    bf(yT_r)[:, d * 2048 + tt * 128: d * 2048 + (tt + 1) * 128],
                                    bf(gateb_r)[:, d * 32:(d + 1) * 32],
                                    start=s, stop=e),
                           reads=["yT", "gateb"], writes=[PSR[bank]], tag="rlog")
                lg = rt[:, 0:32]
                rp = rt[:, 32:64]
                t8 = rt[:, 64:72]
                zap = rt[:, 96:128]
                P.emit("dve", lambda o, bank=bank, lg=lg: o.tensor_copy(lg, ps(bank)[:, 0:32]),
                       reads=[PSR[bank]], writes=["rt"], tag="rcp")
                P.emit("dve", lambda o, lg=lg: o.reduce_max(rt[:, 72:73], lg, axis=AX.X, negate=True),
                       reads=["rt"], writes=["rt"], tag="rmax")
                P.emit("act", lambda o, lg=lg, rp=rp: o.activation(
                    rp, lg, AF.Exp, bias=rt[:, 72:73], accum_out=rt[:, 73:74]),
                    reads=["rt"], writes=["rt"], tag="rexp")
                P.emit("dve", lambda o: o.reciprocal(rt[:, 74:75], rt[:, 73:74]),
                       reads=["rt"], writes=["rt"], tag="rrec")
                P.emit("dve", lambda o, rp=rp: o.tensor_scalar_mul(rp, rp, rt[:, 74:75]),
                       reads=["rt"], writes=["rt"], tag="rnorm")
                P.emit("dve", lambda o, rp=rp, t8=t8: o.max(out=t8, in_=rp),
                       reads=["rt"], writes=["rt"], tag="rtop8")
                P.emit("dve", lambda o: o.memset(rt[:, 68:72], -1e30),
                       reads=["rt"], writes=["rt"], tag="rmemset")
                P.emit("dve", lambda o, rp=rp, t8=t8, zap=zap: o.match_replace(
                    out=zap, in_to_replace=t8, in_values=rp, imm_value=0.0),
                    reads=["rt"], writes=["rt"], tag="rzap")
                P.emit("dve", lambda o, rp=rp, zap=zap: o.tensor_sub(rt[:, 128:160], rp, zap),
                       reads=["rt"], writes=["rt"], tag="rsub")
                P.emit("dve", lambda o: o.reduce_sum(rt[:, 75:76], rt[:, 64:68], axis=AX.X),
                       reads=["rt"], writes=["rt"], tag="rsum4")
                P.emit("dve", lambda o: o.reciprocal(rt[:, 76:77], rt[:, 75:76]),
                       reads=["rt"], writes=["rt"], tag="rrec4")
                P.emit("dve", lambda o, tt=tt: o.tensor_scalar_mul(
                    RW[:, tt * 32:(tt + 1) * 32], rt[:, 128:160], rt[:, 76:77]),
                    reads=["rt"], writes=["RW"], tag="rwst")
        if "rw" in taps:
            P.emit("sp", lambda o: o.dma_start(
                tap_ext["rw"].rearrange("(t p) n -> p t n", p=128),
                RW.ap().rearrange("p (t n) -> p t n", n=32)),
                reads=["RW"], kind="d", tag="taprw")

        if PH >= 5:
            wgu_ = [A.alloc(f"wgu{r}", 8 * 1536) for r in range(2)]
            sg_ = [A.alloc(f"sg{r}", 2 * 2048) for r in range(2)]
            gt_ = [A.alloc(f"gt{r}", 6 * 2048) for r in range(2)]
            for e in range(NE):
                for half in range(2):
                    slot = half
                    dstw2 = bf(wgu_[slot]).rearrange("p (c n) -> p c n", n=1536)
                    srcw2 = gup_in[e].rearrange("(c p) n -> p c n", p=128)
                    for q in range(2):
                        csl = slice(q * 4, q * 4 + 4)
                        ssl = slice(half * 8 + q * 4, half * 8 + q * 4 + 4)
                        P.emit("pool", lambda o, d=dstw2[:, csl, :], s=srcw2[:, ssl, :]:
                               o.dma_start(d, s),
                               writes=[f"wgu{slot}"], kind="d", tag="ld_wgu")
                gc_order = [0, 1, 6, 7, 2, 3, 8, 9, 4, 5, 10, 11]
                for gc in gc_order:
                    bank0 = (gc % 2) * 4
                    for d in range(16):
                        slot = d // 8
                        dl = d % 8
                        for mt in range(4):
                            P.emit("pe", lambda o, gc=gc, dl=dl, d=d, mt=mt, slot=slot, bank0=bank0,
                                   s=(d == 0), e2=(d == 15):
                                   o.matmul(ps(bank0 + mt),
                                            bf(wgu_[slot])[:, dl * 1536 + gc * 128: dl * 1536 + (gc + 1) * 128],
                                            bf(yT_r)[:, d * 2048 + mt * 512: d * 2048 + (mt + 1) * 512],
                                            start=s, stop=e2),
                                   reads=[f"wgu{slot}", "yT"], writes=[PSR[bank0 + mt]], tag="gu")
                    if gc < 6:
                        for mt in range(4):
                            P.emit("act", lambda o, gc=gc, mt=mt, bank0=bank0:
                                   o.activation(f32(sg_[gc % 2])[:, mt * 512:(mt + 1) * 512],
                                                ps(bank0 + mt), AF.Silu),
                                   reads=[PSR[bank0 + mt]], writes=[f"sg{gc % 2}"], tag="silu")
                    else:
                        i = gc - 6
                        for mt in range(4):
                            P.emit("dve", lambda o, i=i, mt=mt, gc=gc, e=e, bank0=bank0:
                                   o.tensor_mul(bf(gt_[e % 2])[:, i * 2048 + mt * 512: i * 2048 + (mt + 1) * 512],
                                                f32(sg_[gc % 2])[:, mt * 512:(mt + 1) * 512],
                                                ps(bank0 + mt)),
                                   reads=[PSR[bank0 + mt], f"sg{gc % 2}"],
                                   writes=[f"gt{e % 2}"], tag="gmul")
                gdst = gated_h[e].rearrange("(c p) n -> p c n", p=128)
                P.emit("sp", lambda o, e=e, gdst=gdst:
                       o.dma_start(gdst, bf(gt_[e % 2]).rearrange("p (c n) -> p c n", n=2048)),
                       reads=[f"gt{e % 2}"], writes=[f"d_g{e}"], kind="d", tag="gst")
        A.release()
        if "gated" in taps:
            for e in range(2):
                P.emit("sp", lambda o, e=e: o.dma_start(tap_ext["gated"][e], gated_h[e]),
                       reads=[f"d_g{e}"], kind="d", tag="tapg")

        A.mark()
        acc_r = A.alloc("acc", 16 * 2 * 2048)

        def acc_ap(tt, csl=slice(0, 2048)):
            a = acc_r[0]
            return arena.ap()[:, a + tt * 4096: a + (tt + 1) * 4096].bitcast(F32)[:, csl]

        if PH >= 6:
            A.mark()
            gl_ = [A.alloc(f"gl{r}", 3 * 2048) for r in range(2)]
            wd_ = [A.alloc(f"wd{r}", 3 * 2048) for r in range(2)]
            for tt in range(16):
                P.emit("dve", lambda o, tt=tt: o.memset(acc_ap(tt), 0.0),
                       writes=[f"acc{tt}"], tag="accz")
            for e in range(NE):
                for half in range(2):
                    slot = half
                    gsrc = gated_h[e].rearrange("(c p) n -> p c n", p=128)
                    P.emit("sp", lambda o, slot=slot, s=gsrc[:, half * 3:half * 3 + 3, :]:
                           o.dma_start(bf(gl_[slot]).rearrange("p (c n) -> p c n", n=2048), s),
                           reads=[f"d_g{e}"], writes=[f"gl{slot}"], kind="d", tag="gld")
                    wsrc = gdn_in[e].rearrange("(c p) n -> p c n", p=128)
                    P.emit("pool", lambda o, slot=slot, s=wsrc[:, half * 3:half * 3 + 3, :]:
                           o.dma_start(bf(wd_[slot]).rearrange("p (c n) -> p c n", n=2048), s),
                           writes=[f"wd{slot}"], kind="d", tag="ld_wd")
                for tt in range(16):
                    grp = (tt % 2) * 4
                    for i in range(6):
                        slot = i // 3
                        il = i % 3
                        for ct in range(4):
                            P.emit("pe", lambda o, il=il, ct=ct, tt=tt, slot=slot, grp=grp,
                                   s=(i == 0), e2=(i == 5):
                                   o.matmul(ps(grp + ct),
                                            bf(gl_[slot])[:, il * 2048 + tt * 128: il * 2048 + (tt + 1) * 128],
                                            bf(wd_[slot])[:, il * 2048 + ct * 512: il * 2048 + (ct + 1) * 512],
                                            start=s, stop=e2),
                                   reads=[f"gl{slot}", f"wd{slot}"], writes=[PSR[grp + ct]], tag="dn")
                    for ct in range(4):
                        P.emit("dve", lambda o, tt=tt, ct=ct, e=e, grp=grp:
                               o.scalar_tensor_tensor(
                                   acc_ap(tt, slice(ct * 512, (ct + 1) * 512)),
                                   ps(grp + ct), RW[:, tt * 32 + e: tt * 32 + e + 1],
                                   acc_ap(tt, slice(ct * 512, (ct + 1) * 512)),
                                   ALU.mult, ALU.add),
                               reads=[PSR[grp + ct], "RW", f"acc{tt}"], writes=[f"acc{tt}"], tag="dacc")
            A.release()
        if "acc" in taps:
            for tt in range(16):
                P.emit("sp", lambda o, tt=tt: o.dma_start(
                    tap_ext["acc"][tt * 128:(tt + 1) * 128, :], acc_ap(tt)),
                    reads=[f"acc{tt}"], kind="d", tag="tapacc")

        if PH >= 7:
            yl_ = [A.alloc(f"yl{r}", 2 * 2048) for r in range(2)]
            h2_ = [A.alloc(f"h2{r}", 2 * 2048) for r in range(2)]
            ob_ = [A.alloc(f"ob{r}", 2 * 2048) for r in range(2)]
            for tt in range(16):
                r = tt % 2
                P.emit("sp", lambda o, tt=tt, r=r: o.dma_start(
                    f32(yl_[r]), y_h[tt * 128:(tt + 1) * 128, :]),
                    reads=["d_y"], writes=[f"yl{r}"], kind="d", tag="yld")
                P.emit("dve", lambda o, tt=tt, r=r:
                       o.scalar_tensor_tensor(f32(h2_[r]), acc_ap(tt), 1.0,
                                              f32(yl_[r]), ALU.mult, ALU.add,
                                              accum_out=stats[:, r * 8:r * 8 + 1]),
                       reads=[f"acc{tt}", f"yl{r}"], writes=[f"h2{r}", f"lnstats{r}"], tag="h2")
                ln_chain(f32(h2_[r]), f"h2{r}", f32(ob_[r]), f"ob{r}", par=r)
                P.emit("sp", lambda o, tt=tt, r=r: o.dma_start(
                    out_ext[tt * 128:(tt + 1) * 128, :], f32(ob_[r])),
                    reads=[f"ob{r}"], writes=["d_out"], kind="d", tag="outst")
        A.release()

        totals = P.finalize()

        def replay(e):
            def go(engobj):
                for op in P.streams[e]:
                    for (snm, val) in op.waits:
                        engobj.wait_ge(sems[snm], val)
                    ins = op.fn(engobj)
                    if op.needs_inc and ins is not None:
                        ins.then_inc(sems[op.inc_sem], 16 if op.kind == "d" else 1)
                if e == "sp":
                    engobj.wait_ge(sems["sd"], totals["sd"])
                    if totals["pd"]:
                        engobj.wait_ge(sems["pd"], totals["pd"])
            return go

        block.tensor(replay("pe"))
        block.vector(replay("dve"))
        block.scalar(replay("act"))
        block.gpsimd(replay("pool"))
        block.sync(replay("sp"))

    return nc


_CACHE = {}


def make_in_maps(L0_attn_x, L0_attn_w_qkv, L0_attn_w_out, L0_ffn_gate,
                 L0_ffn_gate_up_proj, L0_ffn_down_proj):
    x = np.ascontiguousarray(np.asarray(L0_attn_x, dtype=np.float32))
    wqkv = np.asarray(L0_attn_w_qkv, dtype=np.float32)
    wout = np.ascontiguousarray(np.asarray(L0_attn_w_out, dtype=np.float32))
    gate = np.ascontiguousarray(np.asarray(L0_ffn_gate, dtype=np.float32))
    gup = np.ascontiguousarray(np.asarray(L0_ffn_gate_up_proj, dtype=np.float32))
    gdn = np.ascontiguousarray(np.asarray(L0_ffn_down_proj, dtype=np.float32))
    ident, maskc = _host_consts()
    in_maps = []
    for c in range(8):
        b, p = c // 2, c % 2
        wq = wqkv[:, p * 1024:(p + 1) * 1024]
        wk = wqkv[:, 2048 + p * 1024: 2048 + (p + 1) * 1024]
        wv = wqkv[:, 4096 + p * 1024: 4096 + (p + 1) * 1024]
        in_maps.append({
            "x": np.ascontiguousarray(x[b]),
            "xown": np.ascontiguousarray(x[b, p * 2048:(p + 1) * 2048]),
            "wqkv": np.ascontiguousarray(np.concatenate([wq, wk, wv], axis=1)),
            "wout": wout,
            "gate": gate,
            "gup": gup,
            "gdn": gdn,
            "ident": ident,
            "maskc": maskc,
        })
    return in_maps


def kernel(L0_attn_x, L0_attn_w_qkv, L0_attn_w_out, L0_ffn_gate,
           L0_ffn_gate_up_proj, L0_ffn_down_proj):
    if "nc" not in _CACHE:
        _CACHE["nc"] = build_program()
    nc = _CACHE["nc"]
    in_maps = make_in_maps(L0_attn_x, L0_attn_w_qkv, L0_attn_w_out, L0_ffn_gate,
                           L0_ffn_gate_up_proj, L0_ffn_down_proj)
    res = run_bass_kernel_spmd(nc, in_maps, core_ids=list(range(8)))
    out = np.empty((B, S, D), dtype=np.float32)
    for c in range(8):
        b, p = c // 2, c % 2
        out[b, p * 2048:(p + 1) * 2048] = res.results[c]["out"]
    return out


# version 05
# speedup vs baseline: 1.1447x; 1.1447x over previous
import numpy as np
import ml_dtypes
import concourse.bass as bass
import concourse.mybir as mybir
from concourse.bass_utils import run_bass_kernel_spmd

F32 = mybir.dt.float32
BF16 = mybir.dt.bfloat16
AF = mybir.ActivationFunctionType
ALU = mybir.AluOpType
AX = mybir.AxisListType

B, S, D = 4, 4096, 2048
HPC, DH = 8, 128
TPC = 2048
NE, DI = 32, 768
STW = 512
NST = S // STW
C_EXP = 8.0
EPS = 1e-5
QSCALE = 1.0 / float(np.sqrt(np.float32(128.0)))
NEG = -1e10

ENGS = ("pe", "dve", "act", "pool", "sp")
PHASES = ["PA", "PB", "PC", "PD", "PE", "F1", "F2", "PG"]


class Op:
    __slots__ = ("eng", "fn", "reads", "writes", "kind", "idx", "eseq",
                 "waits", "needs_inc", "inc_sem", "inc_amt", "tag")

    def __init__(self, eng, fn, reads, writes, kind, idx, eseq, tag):
        self.eng, self.fn = eng, fn
        self.reads, self.writes = reads, writes
        self.kind = kind
        self.idx, self.eseq = idx, eseq
        self.waits = []
        self.needs_inc = False
        self.inc_sem = None
        self.inc_amt = 0
        self.tag = tag


class Prog:

    def __init__(self):
        self.ops = []
        self.streams = {e: [] for e in ENGS}
        self.regs = {}
        self.clock = {e: {} for e in ENGS}
        self.last_comp = {e: None for e in ENGS}
        self.dma_count = {"sp": 0, "pool": 0}
        self.cc_count = 0

    def region(self, r):
        st = self.regs.get(r)
        if st is None:
            st = [None, []]
            self.regs[r] = st
        return st

    def new_region(self, r, preds=()):
        self.regs[r] = [None, [p for p in preds if p is not None]]

    def _edge(self, op, prod):
        if prod is op:
            return
        e, f = op.eng, prod.eng
        if e == "pe" and f == "pe":
            return
        if self.clock[e].get(f, -1) >= prod.eseq:
            return
        if prod.kind in ("d", "cc"):
            op.waits.append((prod.inc_sem, prod.inc_amt))
            self.clock[e][f] = prod.eseq
            return
        if e == f:
            sub = self.last_comp[e]
            assert sub is not None and sub.eseq >= prod.eseq
            sub.needs_inc = True
            op.waits.append(sub)
            self.clock[e][f] = sub.eseq
        else:
            prod.needs_inc = True
            op.waits.append(prod)
            self.clock[e][f] = prod.eseq

    def emit(self, eng, fn, reads=(), writes=(), kind="c", tag=""):
        idx = len(self.ops)
        op = Op(eng, fn, list(reads), list(writes), kind, idx,
                len(self.streams[eng]), tag)
        self.ops.append(op)
        self.streams[eng].append(op)
        if kind == "d":
            self.dma_count[eng] += 1
            op.inc_sem = "sd" if eng == "sp" else "pd"
            op.inc_amt = 16 * self.dma_count[eng]
            op.needs_inc = True
        elif kind == "cc":
            op.inc_sem = f"cc{self.cc_count}"
            self.cc_count += 1
            op.inc_amt = 1
            op.needs_inc = True
        prods = {}
        for r in op.reads:
            st = self.region(r)
            if st[0] is not None:
                prods[st[0].idx] = st[0]
        for r in op.writes:
            st = self.region(r)
            if st[0] is not None:
                prods[st[0].idx] = st[0]
            for rd in st[1]:
                prods[rd.idx] = rd
        for pidx in sorted(prods):
            self._edge(op, prods[pidx])
        for r in op.reads:
            self.region(r)[1].append(op)
        for r in op.writes:
            st = self.region(r)
            st[0] = op
            st[1] = []
        if kind == "c":
            self.last_comp[eng] = op
        return op

    def finalize(self):
        inc_no = {}
        totals = {}
        for e in ENGS:
            n = 0
            for op in self.streams[e]:
                if op.kind == "c" and op.needs_inc:
                    n += 1
                    op.inc_sem = e
                    inc_no[op.idx] = n
            totals[e] = n
        for op in self.ops:
            byname = {}
            for w in op.waits:
                if isinstance(w, tuple):
                    s, v = w
                else:
                    s, v = w.eng, inc_no[w.idx]
                if byname.get(s, 0) < v:
                    byname[s] = v
            op.waits = sorted(byname.items())
        totals["sd"] = 16 * self.dma_count["sp"]
        totals["pd"] = 16 * self.dma_count["pool"]
        for i in range(self.cc_count):
            totals[f"cc{i}"] = 1
        for op in self.ops:
            for s, v in op.waits:
                assert v <= totals.get(s, 0), f"wait {s}>={v} exceeds total {totals.get(s)} ({op.tag})"
        return totals


class Arena:
    def __init__(self, prog, size):
        self.P = prog
        self.size = size
        self.cur = 0
        self.live = []
        self.marks = []
        self.freed = []

    def alloc(self, name, n):
        a, b = self.cur, self.cur + n
        assert b <= self.size, f"arena overflow: {name}: {b} > {self.size}"
        self.cur = b
        self.live.append((name, a, b))
        preds = []
        st_old = self.P.regs.get(name)
        if st_old is not None:
            preds += ([st_old[0]] if st_old[0] else []) + st_old[1]
        for (nm, fa, fb, fp) in self.freed:
            if fa < b and a < fb:
                preds += fp
        self.P.new_region(name, preds)
        return (a, b)

    def mark(self):
        self.marks.append((self.cur, list(self.live)))

    def release(self):
        cur0, live0 = self.marks.pop()
        for (nm, a, b) in self.live:
            if (nm, a, b) not in live0:
                preds = []
                for rn, st in self.P.regs.items():
                    if rn == nm or rn.startswith(nm):
                        preds += ([st[0]] if st[0] else []) + st[1]
                self.freed.append((nm, a, b, preds))
        self.cur = cur0
        self.live = live0


def _host_consts():
    ident = np.eye(128, dtype=ml_dtypes.bfloat16)
    mask = np.zeros((128, 4 * 512), dtype=np.float32)
    for j in range(4):
        i = np.arange(128)[:, None]
        kk = np.arange(512)[None, :]
        mask[:, j * 512:(j + 1) * 512] = (kk <= 128 * j + i).astype(np.float32)
    return ident, mask.astype(ml_dtypes.bfloat16)


def build_program(upto="PG", taps=()):
    from contextlib import ExitStack
    PH = PHASES.index(upto)
    taps = set(taps)
    nc = bass.Bass("TRN2", target_bir_lowering=False)
    P = Prog()

    x_in = nc.dram_tensor("x", [S, D], F32, kind="ExternalInput").ap()
    xown_in = nc.dram_tensor("xown", [TPC, D], F32, kind="ExternalInput").ap()
    wqkv_in = nc.dram_tensor("wqkv", [D, 3072], F32, kind="ExternalInput").ap()
    wout_in = nc.dram_tensor("wout", [D, D], F32, kind="ExternalInput").ap()
    gate_in = nc.dram_tensor("gate", [D, NE], F32, kind="ExternalInput").ap()
    gup_in = nc.dram_tensor("gup", [NE, D, 2 * DI], F32, kind="ExternalInput").ap()
    gdn_in = nc.dram_tensor("gdn", [NE, DI, D], F32, kind="ExternalInput").ap()
    ident_in = nc.dram_tensor("ident", [128, 128], BF16, kind="ExternalInput").ap()
    maskc_in = nc.dram_tensor("maskc", [128, 2048], BF16, kind="ExternalInput").ap()
    out_ext = nc.dram_tensor("out", [TPC, D], F32, kind="ExternalOutput").ap()

    tap_ext = {}
    tap_shapes = {
        "qT": ([HPC, 128, S], BF16), "kT": ([HPC, 128, S], BF16),
        "v": ([S, HPC * DH], BF16), "ctx": ([HPC * 128, S], BF16),
        "y": ([TPC, D], F32), "rw": ([TPC, NE], F32),
        "gated": ([2, DI, TPC], BF16), "acc": ([TPC, D], F32),
        "ctx2": ([16 * 128, TPC], BF16),
    }
    for t in taps:
        tap_ext[t] = nc.dram_tensor("tap_" + t, *tap_shapes[t], kind="ExternalOutput").ap()

    xbf_h = nc.dram_tensor("xbf_h", [S, D], BF16).ap()
    qT_h = nc.dram_tensor("qT_h", [HPC, 128, S], BF16).ap()
    kT_h = nc.dram_tensor("kT_h", [HPC, 128, S], BF16).ap()
    v_h = nc.dram_tensor("v_h", [S, HPC * DH], BF16).ap()
    y_h = nc.dram_tensor("y_h", [TPC, D], F32).ap()
    ybf_h = nc.dram_tensor("ybf_h", [TPC, D], BF16).ap()
    gated_h = nc.dram_tensor("gated_h", [NE, DI, TPC], BF16).ap()
    ccin = nc.dram_tensor("ccin", [HPC * 128, S // 2], F32)
    ccout = nc.dram_tensor("ccout", [2 * HPC * 128, S // 2], F32)
    ccfo = nc.dram_tensor("ccfo", [64, S // 2], F32)

    with ExitStack() as ctx:
        E = ctx.enter_context
        AR = 102400
        arena = E(nc.sbuf_tensor("arena", [128, AR], BF16))
        ident = E(nc.sbuf_tensor("identt", [128, 128], BF16))
        maskc = E(nc.sbuf_tensor("maskct", [128, 2048], BF16))
        RW = E(nc.sbuf_tensor("RWt", [128, 512], F32))
        stats = E(nc.sbuf_tensor("statst", [128, 16], F32))
        sumblk = E(nc.sbuf_tensor("sumblkt", [128, 48], F32))
        rt = E(nc.sbuf_tensor("rtt", [128, 160], F32))
        cbias = E(nc.sbuf_tensor("cbiast", [128, 2], F32))
        psum = E(nc.psum_tensor("psumt", [128, 4096], F32))

        sems = {}
        for snm in ("pe", "dve", "act", "pool", "sd", "pd", "cc0", "cc1", "cc2", "cc3", "cc4"):
            sems[snm] = E(nc.semaphore("sem_" + snm))
        r_po = E(nc.gpsimd.register("r_po"))
        r_cc = [E(nc.gpsimd.register(f"r_cc{i}")) for i in range(8)]
        block = E(nc.Block())

        A = Arena(P, AR)
        sq_r = [None]

        def ps(i, n=1):
            return psum.ap()[:, i * 512:(i + n) * 512]

        def psb4(i):
            return psum.ap()[:, i * 512:(i + 1) * 512].bitcast(BF16)[:, 0:512]

        PSR = [f"ps{i}" for i in range(8)]

        def bf(rng):
            return arena.ap()[:, rng[0]:rng[1]]

        def f32(rng):
            return arena.ap()[:, rng[0]:rng[1]].bitcast(F32)

        def ln_chain(src_ap, src_reg, dst_ap, dst_reg, par=0):
            q = par * 8
            lr = f"lnstats{par}"
            sqscr = f32(sq_r[0])
            P.emit("dve", lambda o: o.tensor_scalar_mul(stats[:, q + 1:q + 2], stats[:, q:q + 1], -1.0 / 2048.0),
                   reads=[lr], writes=[lr], tag="lnm")
            P.emit("dve", lambda o: o.tensor_scalar_add(src_ap, src_ap, stats[:, q + 1:q + 2]),
                   reads=[lr, src_reg], writes=[src_reg], tag="lnt")
            P.emit("act", lambda o: o.activation(sqscr[:, :], src_ap, AF.Square,
                                                 accum_out=stats[:, q + 2:q + 3]),
                   reads=[src_reg], writes=[f"sqjunk{par}", lr], tag="lnsq")
            P.emit("act", lambda o: o.activation(stats[:, q + 3:q + 4], stats[:, q + 2:q + 3], AF.Ln,
                                                 scale=1.0 / 2048.0, bias=cbias[:, 1:2]),
                   reads=[lr, "cbias"], writes=[lr], tag="lnln")
            P.emit("act", lambda o: o.activation(stats[:, q + 4:q + 5], stats[:, q + 3:q + 4], AF.Exp, scale=-0.5),
                   reads=[lr], writes=[lr], tag="lnexp")
            P.emit("dve", lambda o: o.tensor_scalar_mul(dst_ap, src_ap, stats[:, q + 4:q + 5]),
                   reads=[lr, src_reg], writes=[dst_reg], tag="lny")

        P.emit("dve", lambda o: o.memset(cbias[:, 0:1], -C_EXP),
               writes=["cbias"], tag="cb0")
        P.emit("dve", lambda o: o.memset(cbias[:, 1:2], EPS),
               writes=["cbias"], tag="cb1")
        P.emit("sp", lambda o: o.dma_start(ident[:, :], ident_in),
               writes=["ident"], kind="d", tag="ld_ident")
        P.emit("sp", lambda o: o.dma_start(maskc[:, :], maskc_in),
               writes=["maskc"], kind="d", tag="ld_mask")
        def regsetup(o):
            pid = o.partition_id()
            o.reg_mov(r_po, pid)
            o.reg_mod(r_po, r_po, 2)
            ins = o.reg_mul(r_po, r_po, 1024)
            for k in range(4):
                for rank in range(2):
                    i = k * 2 + rank
                    base = k * 512 * 2048 + rank * 256 * 2048
                    o.reg_mov(r_cc[i], base)
                    ins = o.reg_add(r_cc[i], r_cc[i], r_po)
            return ins
        P.emit("pool", regsetup, tag="regs")

        def xcast(i):
            sl = slice(i * 512, (i + 1) * 512)
            P.emit("pool", lambda o, sl=sl: o.dma_start(xbf_h[sl, :], x_in[sl, :]),
                   writes=[f"d_xbf{i}"], kind="d", tag="xcast")
        xcast(0)
        xcast(1)

        A.mark()
        wq = A.alloc("wq", 16 * 1024)
        wk = A.alloc("wk", 16 * 1024)
        wv = A.alloc("wv", 16 * 1024)
        xT = [A.alloc(f"xT{r}", 16 * 1024) for r in range(2)]
        stage_r = A.alloc("stage", 4 * 512)
        for part, reg, rname in ((0, wq, "wq"), (1, wk, "wk"), (2, wv, "wv")):
            dst = bf(reg).rearrange("p (c n) -> p c n", n=1024)
            src = wqkv_in.rearrange("(c p) n -> p c n", p=128)[:, :, part * 1024:(part + 1) * 1024]
            for cc4 in range(4):
                csl = slice(cc4 * 4, cc4 * 4 + 4)
                P.emit("pool", lambda o, d=dst[:, csl, :], s=src[:, csl, :]: o.dma_start(d, s),
                       writes=[rname], kind="d", tag="ld_wqkv")
        for i in range(2, 8):
            xcast(i)

        def xT_ap(r, d, tsl=slice(0, 1024)):
            a = xT[r][0]
            return arena.ap()[:, a + d * 1024:a + (d + 1) * 1024][:, tsl]

        stg_n = [0]

        def stage_slot():
            i = stg_n[0] % 4
            stg_n[0] += 1
            a = stage_r[0]
            return arena.ap()[:, a + i * 512:a + (i + 1) * 512], f"stg{i}"

        if PH >= 0:
            for sti in range(4):
                r = sti % 2
                t0 = sti * 1024
                for d in range(16):
                    P.emit("sp", lambda o, d=d, r=r, t0=t0: o.dma_start(
                        xT_ap(r, d), xbf_h[t0:t0 + 1024, d * 128:(d + 1) * 128], transpose=True),
                        reads=[f"d_xbf{2 * sti}", f"d_xbf{2 * sti + 1}"],
                        writes=[f"xT{r}"], kind="d", tag="xT")
                for wname, wreg, dest, dreg, scale in (
                        ("wk", wk, kT_h, "d_k", 1.0), ("wq", wq, qT_h, "d_q", QSCALE)):
                    for h in range(HPC):
                        bk0 = (h % 2) * 2
                        for d in range(16):
                            wap = bf(wreg)[:, d * 1024 + h * 128: d * 1024 + (h + 1) * 128]
                            for th in range(2):
                                P.emit("pe", lambda o, w=wap, r=r, d=d, th=th, bk0=bk0,
                                       s=(d == 0), e=(d == 15):
                                       o.matmul(ps(bk0 + th), w,
                                                xT_ap(r, d, slice(th * 512, (th + 1) * 512)),
                                                start=s, stop=e),
                                       reads=[wname, f"xT{r}"], writes=[PSR[bk0 + th]], tag="kqproj")
                        for th in range(2):
                            sap, sreg = stage_slot()
                            P.emit("act", lambda o, sap=sap, bank=bk0 + th, sc=scale:
                                   o.activation(sap, ps(bank), AF.Copy, scale=sc),
                                   reads=[PSR[bk0 + th]], writes=[sreg], tag="kqevac")
                            P.emit("pool", lambda o, sap=sap, h=h, t1=t0 + th * 512, dst=dest:
                                   o.dma_start(dst[h, :, t1:t1 + 512], sap),
                                   reads=[sreg], writes=[dreg], kind="d", tag="kqst")
                for sub in range(8):
                    for vch in range(2):
                        bank = 4 + ((sub * 2 + vch) % 4)
                        tsl = slice(sub * 128, (sub + 1) * 128)
                        for d in range(16):
                            P.emit("pe", lambda o, r=r, d=d, tsl=tsl, vch=vch, bank=bank, s=(d == 0), e=(d == 15):
                                   o.matmul(ps(bank), xT_ap(r, d, tsl),
                                            bf(wv)[:, d * 1024 + vch * 512: d * 1024 + (vch + 1) * 512],
                                            start=s, stop=e),
                                   reads=["wv", f"xT{r}"], writes=[PSR[bank]], tag="vproj")
                        sap, sreg = stage_slot()
                        P.emit("act", lambda o, sap=sap, bank=bank:
                               o.activation(sap, ps(bank), AF.Copy),
                               reads=[PSR[bank]], writes=[sreg], tag="vevac")
                        P.emit("pool", lambda o, sap=sap, t0=t0, sub=sub, vch=vch:
                               o.dma_start(v_h[t0 + sub * 128: t0 + (sub + 1) * 128,
                                               vch * 512:(vch + 1) * 512], sap),
                               reads=[sreg], writes=["d_v"], kind="d", tag="vst")
        A.release()

        for t in ("qT", "kT", "v"):
            if t in taps:
                src = {"qT": qT_h, "kT": kT_h, "v": v_h}[t]
                dreg = {"qT": "d_q", "kT": "d_k", "v": "d_v"}[t]
                P.emit("sp", lambda o, s=src, d2=tap_ext[t]: o.dma_start(d2, s),
                       reads=[dreg], kind="d", tag="tap")

        A.mark()
        ctx_r = A.alloc("ctx", HPC * 4096)
        A.mark()
        kb_ = [A.alloc(f"kb{r}", 4096) for r in range(2)]
        vb_ = [A.alloc(f"vb{r}", 4096) for r in range(2)]
        qb_ = [A.alloc(f"qb{r}", 4096) for r in range(2)]
        Pb_ = [A.alloc(f"P{r}", 4096) for r in range(2)]
        PT_r = A.alloc("PT", 32 * 512)

        if PH >= 1:
            for h in range(HPC):
                r = h % 2
                P.emit("sp", lambda o, h=h, r=r: o.dma_start(bf(kb_[r]), kT_h[h]),
                       reads=["d_k"], writes=[f"kb{r}"], kind="d", tag="kld")
                P.emit("sp", lambda o, h=h, r=r: o.dma_start(bf(qb_[r]), qT_h[h]),
                       reads=["d_q"], writes=[f"qb{r}"], kind="d", tag="qld")
                vsrc = v_h[:, h * 128:(h + 1) * 128].rearrange("(t p) n -> p t n", p=128)
                vdst = bf(vb_[r]).rearrange("p (t n) -> p t n", n=128)
                for q4 in range(4):
                    tsl = slice(q4 * 8, q4 * 8 + 8)
                    P.emit("sp", lambda o, s=vsrc[:, tsl, :], d2=vdst[:, tsl, :]: o.dma_start(d2, s),
                           reads=["d_v"], writes=[f"vb{r}"], kind="d", tag="vld")

                for s_ in range(NST):
                    NB = s_ + 1
                    Q0 = s_ * STW
                    cxb = 4 + (s_ % 2)

                    def pb_scores(j):
                        jp = j % 2
                        nslot = [0]
                        for kb in range(NB):
                            bank = [0, 6][(kb // 2) % 2] + (kb % 2)
                            P.emit("pe", lambda o, j=j, kb=kb, bank=bank, r=r, Q0=Q0:
                                   o.matmul(ps(bank),
                                            bf(qb_[r])[:, Q0 + j * 128: Q0 + (j + 1) * 128],
                                            bf(kb_[r])[:, kb * 512:(kb + 1) * 512],
                                            start=True, stop=True),
                                   reads=[f"qb{r}", f"kb{r}"], writes=[PSR[bank]], tag="sc")
                            diag = (kb == NB - 1)
                            if not diag and (kb % 2 == 1):
                                b0 = [0, 6][((kb - 1) // 2) % 2]
                                sl = nslot[0]; nslot[0] += 1
                                P.emit("act", lambda o, b0=b0, jp=jp, kb=kb, j=j, sl=sl:
                                       o.activation(bf(Pb_[jp])[:, (kb - 1) * 512:(kb + 1) * 512],
                                                    ps(b0, 2), AF.Exp, bias=cbias[:, 0:1],
                                                    accum_out=sumblk[:, j * 8 + sl: j * 8 + sl + 1]),
                                       reads=[PSR[b0], PSR[b0 + 1], "cbias"],
                                       writes=[f"P{jp}", "sumblk"], tag="exp2")
                            elif not diag and kb == NB - 2:
                                sl = nslot[0]; nslot[0] += 1
                                P.emit("act", lambda o, bank=bank, jp=jp, kb=kb, j=j, sl=sl:
                                       o.activation(bf(Pb_[jp])[:, kb * 512:(kb + 1) * 512],
                                                    ps(bank), AF.Exp, bias=cbias[:, 0:1],
                                                    accum_out=sumblk[:, j * 8 + sl: j * 8 + sl + 1]),
                                       reads=[PSR[bank], "cbias"],
                                       writes=[f"P{jp}", "sumblk"], tag="exp1")
                            elif diag:
                                P.emit("act", lambda o, bank=bank, jp=jp, kb=kb:
                                       o.activation(bf(Pb_[jp])[:, kb * 512:(kb + 1) * 512],
                                                    ps(bank), AF.Exp, bias=cbias[:, 0:1]),
                                       reads=[PSR[bank], "cbias"], writes=[f"P{jp}"], tag="expd")
                                sl = nslot[0]; nslot[0] += 1
                                P.emit("dve", lambda o, jp=jp, kb=kb, j=j, sl=sl:
                                       o.scalar_tensor_tensor(
                                           bf(Pb_[jp])[:, kb * 512:(kb + 1) * 512],
                                           bf(Pb_[jp])[:, kb * 512:(kb + 1) * 512], 1.0,
                                           maskc[:, j * 512:(j + 1) * 512],
                                           ALU.mult, ALU.mult,
                                           accum_out=sumblk[:, j * 8 + sl: j * 8 + sl + 1]),
                                       reads=[f"P{jp}", "maskc"],
                                       writes=[f"P{jp}", "sumblk"], tag="maskd")
                        return nslot[0]

                    def pb_post(j, nslots):
                        jp = j % 2
                        P.emit("dve", lambda o, j=j, n=nslots:
                               o.reduce_sum(rt[:, 128 + j: 129 + j], sumblk[:, j * 8: j * 8 + n], axis=AX.X),
                               reads=["sumblk"], writes=["rt"], tag="psum")
                        P.emit("dve", lambda o, j=j:
                               o.reciprocal(rt[:, 132 + j: 133 + j], rt[:, 128 + j: 129 + j]),
                               reads=["rt"], writes=["rt"], tag="prec")
                        P.emit("dve", lambda o, j=j, jp=jp, NB=NB:
                               o.tensor_scalar_mul(bf(Pb_[jp])[:, 0:NB * 512],
                                                   bf(Pb_[jp])[:, 0:NB * 512],
                                                   rt[:, 132 + j: 133 + j]),
                               reads=["rt", f"P{jp}"], writes=[f"P{jp}"], tag="pscale")

                    def pb_tr(j):
                        jp = j % 2
                        for kbq in range(NB):
                            bank = 2 + (kbq % 2)
                            for ii in range(4):
                                kb = kbq * 4 + ii
                                P.emit("pe", lambda o, jp=jp, kb=kb, bank=bank, ii=ii:
                                       o.transpose(psb4(bank)[:, ii * 128:(ii + 1) * 128],
                                                   bf(Pb_[jp])[:, kb * 128:(kb + 1) * 128],
                                                   ident[:, :]),
                                       reads=[f"P{jp}", "ident"], writes=[PSR[bank]], tag="tr")
                            dst = bf(PT_r).rearrange("p (k n) -> p k n", n=512)
                            P.emit("dve", lambda o, j=j, kbq=kbq, bank=bank, dst=dst:
                                   o.tensor_copy(
                                       dst[:, kbq * 4:(kbq + 1) * 4, j * 128:(j + 1) * 128],
                                       psb4(bank).rearrange("p (k n) -> p k n", n=128)),
                                   reads=[PSR[bank]], writes=["PT"], tag="trev")

                    def pb_av():
                        for kb in range(4 * NB):
                            P.emit("pe", lambda o, kb=kb, r=r, s4=(kb == 0), e4=(kb == 4 * NB - 1), bank=cxb:
                                   o.matmul(ps(bank),
                                            bf(vb_[r])[:, kb * 128:(kb + 1) * 128],
                                            bf(PT_r)[:, kb * 512:(kb + 1) * 512],
                                            start=s4, stop=e4),
                                   reads=[f"vb{r}", "PT"], writes=[PSR[cxb]], tag="av")
                        P.emit("dve", lambda o, h=h, Q0=Q0, bank=cxb:
                               o.tensor_copy(bf(ctx_r)[:, h * 4096 + Q0: h * 4096 + Q0 + STW],
                                             ps(bank)),
                               reads=[PSR[cxb]], writes=[f"ctx{h}"], tag="ctxev")

                    ns0 = pb_scores(0)
                    ns1 = pb_scores(1)
                    pb_post(0, ns0)
                    pb_tr(0)
                    ns2 = pb_scores(2)
                    pb_post(1, ns1)
                    pb_tr(1)
                    ns3 = pb_scores(3)
                    pb_post(2, ns2)
                    pb_tr(2)
                    pb_post(3, ns3)
                    pb_tr(3)
                    pb_av()

                if PH >= 2 and h % 2 == 1:
                    k = h // 2
                    for hh in (h - 1, h):
                        P.emit("sp", lambda o, hh=hh: o.dma_start(
                            ccin.ap()[hh * 128:(hh + 1) * 128, :],
                            bf(ctx_r)[:, hh * 4096:(hh + 1) * 4096].bitcast(F32)),
                            reads=[f"ctx{hh}"], writes=[f"d_ccin{k}", f"ctxg{k}"],
                            kind="d", tag="ccst")
                    P.emit("pool", lambda o, k=k: o.collective_compute(
                        "AllGather", ALU.bypass,
                        replica_groups=[[0, 1], [2, 3], [4, 5], [6, 7]],
                        ins=[ccin.ap()[k * 256:(k + 1) * 256, :].opt()],
                        outs=[ccout.ap()[k * 512:(k + 1) * 512, :].opt()]),
                        reads=[f"d_ccin{k}"], writes=[f"d_ccout{k}"], kind="cc", tag="ccg")
                    if k == 3:
                        P.emit("pool", lambda o: o.collective_compute(
                            "AllGather", ALU.bypass,
                            replica_groups=[[0, 1], [2, 3], [4, 5], [6, 7]],
                            ins=[ccin.ap()[0:32, :].opt()],
                            outs=[ccfo.ap().opt()]),
                            reads=["d_ccin0"], writes=["d_ccfo"], kind="cc", tag="ccfence")
        A.release()

        if "ctx" in taps:
            for h in range(HPC):
                P.emit("sp", lambda o, h=h: o.dma_start(
                    tap_ext["ctx"][h * 128:(h + 1) * 128, :], bf(ctx_r)[:, h * 4096:(h + 1) * 4096]),
                    reads=[f"ctx{h}"], kind="d", tag="tapctx")

        def ctx2_chunk(c):
            k = (c % 8) // 2
            pos = (c % 8) % 2 + 2 * (c // 8)
            a = ctx_r[0] + 2 * k * 4096 + pos * 2048
            return arena.ap()[:, a:a + 2048]

        if PH >= 2:
            for k in range(4):
                a = ctx_r[0] + 2 * k * 4096
                dstk = arena.ap()[:, a:a + 4 * 2048].bitcast(F32).rearrange(
                    "p (c n) -> p c n", n=1024)
                for rank in range(2):
                    src = bass.AP(ccout, r_cc[k * 2 + rank],
                                  [[2048, 128], [128 * 2048, 2], [1, 1024]])
                    P.emit("pool", lambda o, d=dstk[:, rank * 2:rank * 2 + 2, :], s=src:
                           o.dma_start(d, s),
                           reads=["d_ccout0", "d_ccout1", "d_ccout2", "d_ccout3", "d_ccfo"],
                           writes=[f"ctxg{k}"], kind="d", tag="ccld")
        if "ctx2" in taps:
            for c in range(16):
                P.emit("sp", lambda o, c=c: o.dma_start(
                    tap_ext["ctx2"][c * 128:(c + 1) * 128, :], ctx2_chunk(c)),
                    reads=[f"ctxg{(c % 8) // 2}"], kind="d", tag="tapctx2")

        if PH >= 3:
            sq_r[0] = A.alloc("sqscrA", 2 * 2048)
            wout_r = A.alloc("wout", 16 * 2048)
            xs_ = [A.alloc(f"xs{r}", 2 * 2048) for r in range(2)]
            h1_ = [A.alloc(f"h1{r}", 2 * 2048) for r in range(2)]
            y_ = [A.alloc(f"yb{r}", 2 * 2048) for r in range(2)]
            dstw = bf(wout_r).rearrange("p (c n) -> p c n", n=2048)
            srcw = wout_in.rearrange("(c p) n -> p c n", p=128)
            for cc4 in range(4):
                csl = slice(cc4 * 4, cc4 * 4 + 4)
                P.emit("pool", lambda o, d=dstw[:, csl, :], s=srcw[:, csl, :]: o.dma_start(d, s),
                       writes=["wout"], kind="d", tag="ld_wout")
            for tt in range(16):
                r = tt % 2
                grp = 0 if r == 0 else 4
                banks = [PSR[grp + i] for i in range(4)]
                P.emit("sp", lambda o, tt=tt, r=r: o.dma_start(
                    f32(xs_[r]), xown_in[tt * 128:(tt + 1) * 128, :]),
                    writes=[f"xs{r}"], kind="d", tag="xld")
                for d in range(16):
                    for ct in range(4):
                        P.emit("pe", lambda o, d=d, ct=ct, tt=tt, grp=grp, s=(d == 0), e=(d == 15):
                               o.matmul(ps(grp + ct),
                                        ctx2_chunk(d)[:, tt * 128:(tt + 1) * 128],
                                        bf(wout_r)[:, d * 2048 + ct * 512: d * 2048 + (ct + 1) * 512],
                                        start=s, stop=e),
                               reads=[f"ctxg{(d % 8) // 2}", "wout"], writes=[banks[ct]], tag="oproj")
                P.emit("dve", lambda o, r=r, grp=grp:
                       o.scalar_tensor_tensor(f32(h1_[r]), ps(grp, 4), 1.0, f32(xs_[r]),
                                              ALU.mult, ALU.add, accum_out=stats[:, r * 8:r * 8 + 1]),
                       reads=banks + [f"xs{r}"], writes=[f"h1{r}", f"lnstats{r}"], tag="h1")
                ln_chain(f32(h1_[r]), f"h1{r}", f32(y_[r]), f"yb{r}", par=r)
                P.emit("sp", lambda o, tt=tt, r=r: o.dma_start(
                    y_h[tt * 128:(tt + 1) * 128, :], f32(y_[r])),
                    reads=[f"yb{r}"], writes=["d_y"], kind="d", tag="yst")
                P.emit("pool", lambda o, tt=tt, r=r: o.dma_start(
                    ybf_h[tt * 128:(tt + 1) * 128, :], f32(y_[r])),
                    reads=[f"yb{r}"], writes=["d_ybf"], kind="d", tag="ybfst")
        A.release()

        if "y" in taps:
            P.emit("sp", lambda o: o.dma_start(tap_ext["y"], y_h),
                   reads=["d_y"], kind="d", tag="tapy")

        A.mark()
        yT_r = A.alloc("yT", 16 * 2048)
        if PH >= 4:
            gateb_r = A.alloc("gateb", 16 * 32)
            for d in range(16):
                P.emit("sp", lambda o, d=d: o.dma_start(
                    bf(yT_r)[:, d * 2048:(d + 1) * 2048],
                    ybf_h[:, d * 128:(d + 1) * 128], transpose=True),
                    reads=["d_ybf"], writes=["yT"], kind="d", tag="yT")
            P.emit("pool", lambda o: o.dma_start(
                bf(gateb_r).rearrange("p (c n) -> p c n", n=32),
                gate_in.rearrange("(c p) n -> p c n", p=128)),
                writes=["gateb"], kind="d", tag="ld_gate")
            for tt in range(16):
                bank = 6 + tt % 2
                for d in range(16):
                    P.emit("pe", lambda o, d=d, tt=tt, bank=bank, s=(d == 0), e=(d == 15):
                           o.matmul(ps(bank)[:, 0:32],
                                    bf(yT_r)[:, d * 2048 + tt * 128: d * 2048 + (tt + 1) * 128],
                                    bf(gateb_r)[:, d * 32:(d + 1) * 32],
                                    start=s, stop=e),
                           reads=["yT", "gateb"], writes=[PSR[bank]], tag="rlog")
                lg = rt[:, 0:32]
                rp = rt[:, 32:64]
                t8 = rt[:, 64:72]
                zap = rt[:, 96:128]
                P.emit("dve", lambda o, bank=bank, lg=lg: o.tensor_copy(lg, ps(bank)[:, 0:32]),
                       reads=[PSR[bank]], writes=["rt"], tag="rcp")
                P.emit("dve", lambda o, lg=lg: o.reduce_max(rt[:, 72:73], lg, axis=AX.X, negate=True),
                       reads=["rt"], writes=["rt"], tag="rmax")
                P.emit("act", lambda o, lg=lg, rp=rp: o.activation(
                    rp, lg, AF.Exp, bias=rt[:, 72:73], accum_out=rt[:, 73:74]),
                    reads=["rt"], writes=["rt"], tag="rexp")
                P.emit("dve", lambda o: o.reciprocal(rt[:, 74:75], rt[:, 73:74]),
                       reads=["rt"], writes=["rt"], tag="rrec")
                P.emit("dve", lambda o, rp=rp: o.tensor_scalar_mul(rp, rp, rt[:, 74:75]),
                       reads=["rt"], writes=["rt"], tag="rnorm")
                P.emit("dve", lambda o, rp=rp, t8=t8: o.max(out=t8, in_=rp),
                       reads=["rt"], writes=["rt"], tag="rtop8")
                P.emit("dve", lambda o: o.memset(rt[:, 68:72], -1e30),
                       reads=["rt"], writes=["rt"], tag="rmemset")
                P.emit("dve", lambda o, rp=rp, t8=t8, zap=zap: o.match_replace(
                    out=zap, in_to_replace=t8, in_values=rp, imm_value=0.0),
                    reads=["rt"], writes=["rt"], tag="rzap")
                P.emit("dve", lambda o, rp=rp, zap=zap: o.tensor_sub(rt[:, 128:160], rp, zap),
                       reads=["rt"], writes=["rt"], tag="rsub")
                P.emit("dve", lambda o: o.reduce_sum(rt[:, 75:76], rt[:, 64:68], axis=AX.X),
                       reads=["rt"], writes=["rt"], tag="rsum4")
                P.emit("dve", lambda o: o.reciprocal(rt[:, 76:77], rt[:, 75:76]),
                       reads=["rt"], writes=["rt"], tag="rrec4")
                P.emit("dve", lambda o, tt=tt: o.tensor_scalar_mul(
                    RW[:, tt * 32:(tt + 1) * 32], rt[:, 128:160], rt[:, 76:77]),
                    reads=["rt"], writes=["RW"], tag="rwst")
        if "rw" in taps:
            P.emit("sp", lambda o: o.dma_start(
                tap_ext["rw"].rearrange("(t p) n -> p t n", p=128),
                RW.ap().rearrange("p (t n) -> p t n", n=32)),
                reads=["RW"], kind="d", tag="taprw")

        if PH >= 5:
            wguA = A.alloc("wguA", 16 * 768)
            wguB = [A.alloc(f"wguB{r}", 16 * 768) for r in range(2)]
            sg_r = A.alloc("sg", 6 * 2048)
            gt_r = A.alloc("gt", 6 * 2048)

            def wgu_load(e, half):
                srcw2 = gup_in[e].rearrange("(c p) n -> p c n", p=128)
                reg, rname = (wguA, "wguA") if half == 0 else (wguB[e % 2], f"wguB{e % 2}")
                dstw2 = bf(reg).rearrange("p (c n) -> p c n", n=768)
                for q in range(2):
                    csl = slice(q * 8, q * 8 + 8)
                    P.emit("pool", lambda o, d=dstw2[:, csl, :],
                           s=srcw2[:, csl, half * 768:(half + 1) * 768]:
                           o.dma_start(d, s),
                           writes=[rname], kind="d", tag="ld_wgu")
            wgu_load(0, 0)
            wgu_load(0, 1)
            wgu_load(1, 1)
            for e in range(NE):
                for gci, gc in enumerate(list(range(6)) + list(range(6, 12))):
                    bank0 = (gc % 2) * 4
                    wreg, wr = (wguA, "wguA") if gc < 6 else (wguB[e % 2], f"wguB{e % 2}")
                    gl_ = gc if gc < 6 else gc - 6
                    for d in range(16):
                        for mt in range(4):
                            P.emit("pe", lambda o, gl_=gl_, d=d, mt=mt, wreg=wreg, bank0=bank0,
                                   s=(d == 0), e2=(d == 15):
                                   o.matmul(ps(bank0 + mt),
                                            bf(wreg)[:, d * 768 + gl_ * 128: d * 768 + (gl_ + 1) * 128],
                                            bf(yT_r)[:, d * 2048 + mt * 512: d * 2048 + (mt + 1) * 512],
                                            start=s, stop=e2),
                                   reads=[wr, "yT"], writes=[PSR[bank0 + mt]], tag="gu")
                    if gci == 5 and e + 1 < NE:
                        wgu_load(e + 1, 0)
                    if gci == 11 and e + 2 < NE:
                        wgu_load(e + 2, 1)
                    if gc < 6:
                        for mt in range(4):
                            P.emit("act", lambda o, gc=gc, mt=mt, bank0=bank0:
                                   o.activation(bf(sg_r)[:, gc * 2048 + mt * 512: gc * 2048 + (mt + 1) * 512],
                                                ps(bank0 + mt), AF.Silu),
                                   reads=[PSR[bank0 + mt]], writes=["sg"], tag="silu")
                    else:
                        i = gc - 6
                        for mt in range(4):
                            P.emit("dve", lambda o, i=i, mt=mt, bank0=bank0:
                                   o.tensor_mul(bf(gt_r)[:, i * 2048 + mt * 512: i * 2048 + (mt + 1) * 512],
                                                bf(sg_r)[:, i * 2048 + mt * 512: i * 2048 + (mt + 1) * 512],
                                                ps(bank0 + mt)),
                                   reads=[PSR[bank0 + mt], "sg"],
                                   writes=[f"gt{i}"], tag="gmul")
                        P.emit("sp", lambda o, e=e, i=i:
                               o.dma_start(gated_h[e][i * 128:(i + 1) * 128, :],
                                           bf(gt_r)[:, i * 2048:(i + 1) * 2048]),
                               reads=[f"gt{i}"], writes=[f"d_g{e}"], kind="d", tag="gst")
        A.release()
        if "gated" in taps:
            for e in range(2):
                P.emit("sp", lambda o, e=e: o.dma_start(tap_ext["gated"][e], gated_h[e]),
                       reads=[f"d_g{e}"], kind="d", tag="tapg")

        A.mark()
        acc_r = A.alloc("acc", 16 * 2 * 2048)

        def acc_ap(tt, csl=slice(0, 2048)):
            a = acc_r[0]
            return arena.ap()[:, a + tt * 4096: a + (tt + 1) * 4096].bitcast(F32)[:, csl]

        if PH >= 6:
            A.mark()
            gl_ = [A.alloc(f"gl{r}", 3 * 2048) for r in range(2)]
            wd_ = [A.alloc(f"wd{r}", 6 * 2048) for r in range(2)]
            for tt in range(16):
                P.emit("dve", lambda o, tt=tt: o.memset(acc_ap(tt), 0.0),
                       writes=[f"acc{tt}"], tag="accz")

            def wd_load(e):
                slot = e % 2
                wsrc = gdn_in[e].rearrange("(c p) n -> p c n", p=128)
                for half in range(2):
                    P.emit("pool", lambda o, slot=slot, s=wsrc[:, half * 3:half * 3 + 3, :],
                           half=half:
                           o.dma_start(bf(wd_[slot]).rearrange("p (c n) -> p c n", n=2048)[:, half * 3:half * 3 + 3, :], s),
                           writes=[f"wd{slot}"], kind="d", tag="ld_wd")
            wd_load(0)
            for e in range(NE):
                for half in range(2):
                    gsrc = gated_h[e].rearrange("(c p) n -> p c n", p=128)
                    P.emit("sp", lambda o, slot=half, s=gsrc[:, half * 3:half * 3 + 3, :]:
                           o.dma_start(bf(gl_[slot]).rearrange("p (c n) -> p c n", n=2048), s),
                           reads=[f"d_g{e}"], writes=[f"gl{half}"], kind="d", tag="gld")
                if e + 1 < NE:
                    wd_load(e + 1)
                for tt in range(16):
                    grp = (tt % 2) * 4
                    for i in range(6):
                        slot = i // 3
                        il = i % 3
                        for ct in range(4):
                            P.emit("pe", lambda o, i=i, il=il, ct=ct, tt=tt, slot=slot, grp=grp,
                                   ws=e % 2, s=(i == 0), e2=(i == 5):
                                   o.matmul(ps(grp + ct),
                                            bf(gl_[slot])[:, il * 2048 + tt * 128: il * 2048 + (tt + 1) * 128],
                                            bf(wd_[ws])[:, i * 2048 + ct * 512: i * 2048 + (ct + 1) * 512],
                                            start=s, stop=e2),
                                   reads=[f"gl{slot}", f"wd{e % 2}"], writes=[PSR[grp + ct]], tag="dn")
                    for ct in range(4):
                        P.emit("dve", lambda o, tt=tt, ct=ct, e=e, grp=grp:
                               o.scalar_tensor_tensor(
                                   acc_ap(tt, slice(ct * 512, (ct + 1) * 512)),
                                   ps(grp + ct), RW[:, tt * 32 + e: tt * 32 + e + 1],
                                   acc_ap(tt, slice(ct * 512, (ct + 1) * 512)),
                                   ALU.mult, ALU.add),
                               reads=[PSR[grp + ct], "RW", f"acc{tt}"], writes=[f"acc{tt}"], tag="dacc")
            A.release()
        if "acc" in taps:
            for tt in range(16):
                P.emit("sp", lambda o, tt=tt: o.dma_start(
                    tap_ext["acc"][tt * 128:(tt + 1) * 128, :], acc_ap(tt)),
                    reads=[f"acc{tt}"], kind="d", tag="tapacc")

        if PH >= 7:
            sq_r[0] = A.alloc("sqscrB", 2 * 2048)
            yl_ = [A.alloc(f"yl{r}", 2 * 2048) for r in range(2)]
            h2_ = [A.alloc(f"h2{r}", 2 * 2048) for r in range(2)]
            ob_ = [A.alloc(f"ob{r}", 2 * 2048) for r in range(2)]
            for tt in range(16):
                r = tt % 2
                P.emit("sp", lambda o, tt=tt, r=r: o.dma_start(
                    f32(yl_[r]), y_h[tt * 128:(tt + 1) * 128, :]),
                    reads=["d_y"], writes=[f"yl{r}"], kind="d", tag="yld")
                P.emit("dve", lambda o, tt=tt, r=r:
                       o.scalar_tensor_tensor(f32(h2_[r]), acc_ap(tt), 1.0,
                                              f32(yl_[r]), ALU.mult, ALU.add,
                                              accum_out=stats[:, r * 8:r * 8 + 1]),
                       reads=[f"acc{tt}", f"yl{r}"], writes=[f"h2{r}", f"lnstats{r}"], tag="h2")
                ln_chain(f32(h2_[r]), f"h2{r}", f32(ob_[r]), f"ob{r}", par=r)
                P.emit("sp", lambda o, tt=tt, r=r: o.dma_start(
                    out_ext[tt * 128:(tt + 1) * 128, :], f32(ob_[r])),
                    reads=[f"ob{r}"], writes=["d_out"], kind="d", tag="outst")
        A.release()

        totals = P.finalize()

        def replay(e):
            def go(engobj):
                for op in P.streams[e]:
                    for (snm, val) in op.waits:
                        engobj.wait_ge(sems[snm], val)
                    ins = op.fn(engobj)
                    if op.needs_inc and ins is not None:
                        ins.then_inc(sems[op.inc_sem], 16 if op.kind == "d" else 1)
                if e == "sp":
                    engobj.wait_ge(sems["sd"], totals["sd"])
                    if totals["pd"]:
                        engobj.wait_ge(sems["pd"], totals["pd"])
            return go

        block.tensor(replay("pe"))
        block.vector(replay("dve"))
        block.scalar(replay("act"))
        block.gpsimd(replay("pool"))
        block.sync(replay("sp"))

    return nc


_CACHE = {}


def make_in_maps(L0_attn_x, L0_attn_w_qkv, L0_attn_w_out, L0_ffn_gate,
                 L0_ffn_gate_up_proj, L0_ffn_down_proj):
    x = np.ascontiguousarray(np.asarray(L0_attn_x, dtype=np.float32))
    wqkv = np.asarray(L0_attn_w_qkv, dtype=np.float32)
    wout = np.ascontiguousarray(np.asarray(L0_attn_w_out, dtype=np.float32))
    gate = np.ascontiguousarray(np.asarray(L0_ffn_gate, dtype=np.float32))
    gup = np.ascontiguousarray(np.asarray(L0_ffn_gate_up_proj, dtype=np.float32))
    gdn = np.ascontiguousarray(np.asarray(L0_ffn_down_proj, dtype=np.float32))
    ident, maskc = _host_consts()
    in_maps = []
    for c in range(8):
        b, p = c // 2, c % 2
        wq = wqkv[:, p * 1024:(p + 1) * 1024]
        wk = wqkv[:, 2048 + p * 1024: 2048 + (p + 1) * 1024]
        wv = wqkv[:, 4096 + p * 1024: 4096 + (p + 1) * 1024]
        in_maps.append({
            "x": np.ascontiguousarray(x[b]),
            "xown": np.ascontiguousarray(x[b, p * 2048:(p + 1) * 2048]),
            "wqkv": np.ascontiguousarray(np.concatenate([wq, wk, wv], axis=1)),
            "wout": wout,
            "gate": gate,
            "gup": gup,
            "gdn": gdn,
            "ident": ident,
            "maskc": maskc,
        })
    return in_maps


def kernel(L0_attn_x, L0_attn_w_qkv, L0_attn_w_out, L0_ffn_gate,
           L0_ffn_gate_up_proj, L0_ffn_down_proj):
    if "nc" not in _CACHE:
        _CACHE["nc"] = build_program()
    nc = _CACHE["nc"]
    in_maps = make_in_maps(L0_attn_x, L0_attn_w_qkv, L0_attn_w_out, L0_ffn_gate,
                           L0_ffn_gate_up_proj, L0_ffn_down_proj)
    res = run_bass_kernel_spmd(nc, in_maps, core_ids=list(range(8)))
    out = np.empty((B, S, D), dtype=np.float32)
    for c in range(8):
        b, p = c // 2, c % 2
        out[b, p * 2048:(p + 1) * 2048] = res.results[c]["out"]
    return out


# version 06
# speedup vs baseline: 1.1857x; 1.0357x over previous
import numpy as np
import ml_dtypes
import concourse.bass as bass
import concourse.mybir as mybir
from concourse.bass_utils import run_bass_kernel_spmd

F32 = mybir.dt.float32
BF16 = mybir.dt.bfloat16
AF = mybir.ActivationFunctionType
ALU = mybir.AluOpType
AX = mybir.AxisListType

B, S, D = 4, 4096, 2048
HPC, DH = 8, 128
TPC = 2048
NE, DI = 32, 768
STW = 512
NST = S // STW
C_EXP = 8.0
EPS = 1e-5
QSCALE = 1.0 / float(np.sqrt(np.float32(128.0)))
NEG = -1e10

ENGS = ("pe", "dve", "act", "pool", "sp")
PHASES = ["PA", "PB", "PC", "PD", "PE", "F1", "F2", "PG"]


class Op:
    __slots__ = ("eng", "fn", "reads", "writes", "kind", "idx", "eseq",
                 "waits", "needs_inc", "inc_sem", "inc_amt", "tag")

    def __init__(self, eng, fn, reads, writes, kind, idx, eseq, tag):
        self.eng, self.fn = eng, fn
        self.reads, self.writes = reads, writes
        self.kind = kind
        self.idx, self.eseq = idx, eseq
        self.waits = []
        self.needs_inc = False
        self.inc_sem = None
        self.inc_amt = 0
        self.tag = tag


class Prog:

    def __init__(self):
        self.ops = []
        self.streams = {e: [] for e in ENGS}
        self.regs = {}
        self.clock = {e: {} for e in ENGS}
        self.last_comp = {e: None for e in ENGS}
        self.dma_count = {"sp": 0, "pool": 0}
        self.cc_count = 0

    def region(self, r):
        st = self.regs.get(r)
        if st is None:
            st = [None, []]
            self.regs[r] = st
        return st

    def new_region(self, r, preds=()):
        self.regs[r] = [None, [p for p in preds if p is not None]]

    def _edge(self, op, prod):
        if prod is op:
            return
        e, f = op.eng, prod.eng
        if e == "pe" and f == "pe":
            return
        if self.clock[e].get(f, -1) >= prod.eseq:
            return
        if prod.kind in ("d", "cc"):
            op.waits.append((prod.inc_sem, prod.inc_amt))
            self.clock[e][f] = prod.eseq
            return
        if e == f:
            sub = self.last_comp[e]
            assert sub is not None and sub.eseq >= prod.eseq
            sub.needs_inc = True
            op.waits.append(sub)
            self.clock[e][f] = sub.eseq
        else:
            prod.needs_inc = True
            op.waits.append(prod)
            self.clock[e][f] = prod.eseq

    def emit(self, eng, fn, reads=(), writes=(), kind="c", tag=""):
        idx = len(self.ops)
        op = Op(eng, fn, list(reads), list(writes), kind, idx,
                len(self.streams[eng]), tag)
        self.ops.append(op)
        self.streams[eng].append(op)
        if kind == "d":
            self.dma_count[eng] += 1
            op.inc_sem = "sd" if eng == "sp" else "pd"
            op.inc_amt = 16 * self.dma_count[eng]
            op.needs_inc = True
        elif kind == "cc":
            op.inc_sem = f"cc{self.cc_count}"
            self.cc_count += 1
            op.inc_amt = 1
            op.needs_inc = True
        prods = {}
        for r in op.reads:
            st = self.region(r)
            if st[0] is not None:
                prods[st[0].idx] = st[0]
        for r in op.writes:
            st = self.region(r)
            if st[0] is not None:
                prods[st[0].idx] = st[0]
            for rd in st[1]:
                prods[rd.idx] = rd
        for pidx in sorted(prods):
            self._edge(op, prods[pidx])
        for r in op.reads:
            self.region(r)[1].append(op)
        for r in op.writes:
            st = self.region(r)
            st[0] = op
            st[1] = []
        if kind == "c":
            self.last_comp[eng] = op
        return op

    def finalize(self):
        inc_no = {}
        totals = {}
        for e in ENGS:
            n = 0
            for op in self.streams[e]:
                if op.kind == "c" and op.needs_inc:
                    n += 1
                    op.inc_sem = e
                    inc_no[op.idx] = n
            totals[e] = n
        for op in self.ops:
            byname = {}
            for w in op.waits:
                if isinstance(w, tuple):
                    s, v = w
                else:
                    s, v = w.eng, inc_no[w.idx]
                if byname.get(s, 0) < v:
                    byname[s] = v
            op.waits = sorted(byname.items())
        totals["sd"] = 16 * self.dma_count["sp"]
        totals["pd"] = 16 * self.dma_count["pool"]
        for i in range(self.cc_count):
            totals[f"cc{i}"] = 1
        for op in self.ops:
            for s, v in op.waits:
                assert v <= totals.get(s, 0), f"wait {s}>={v} exceeds total {totals.get(s)} ({op.tag})"
        return totals


class Arena:
    def __init__(self, prog, size):
        self.P = prog
        self.size = size
        self.cur = 0
        self.live = []
        self.marks = []
        self.freed = []

    def alloc(self, name, n):
        a, b = self.cur, self.cur + n
        assert b <= self.size, f"arena overflow: {name}: {b} > {self.size}"
        self.cur = b
        self.live.append((name, a, b))
        preds = []
        st_old = self.P.regs.get(name)
        if st_old is not None:
            preds += ([st_old[0]] if st_old[0] else []) + st_old[1]
        for (nm, fa, fb, fp) in self.freed:
            if fa < b and a < fb:
                preds += fp
        self.P.new_region(name, preds)
        return (a, b)

    def mark(self):
        self.marks.append((self.cur, list(self.live)))

    def release(self):
        cur0, live0 = self.marks.pop()
        for (nm, a, b) in self.live:
            if (nm, a, b) not in live0:
                preds = []
                for rn, st in self.P.regs.items():
                    if rn == nm or rn.startswith(nm):
                        preds += ([st[0]] if st[0] else []) + st[1]
                self.freed.append((nm, a, b, preds))
        self.cur = cur0
        self.live = live0


def _host_consts():
    ident = np.eye(128, dtype=ml_dtypes.bfloat16)
    mask = np.zeros((128, 4 * 512), dtype=np.float32)
    for m in range(4):
        i = np.arange(128)[:, None]
        q = np.arange(512)[None, :]
        mask[:, m * 512:(m + 1) * 512] = (q >= m * 128 + i).astype(np.float32)
    return ident, mask.astype(ml_dtypes.bfloat16)


def build_program(upto="PG", taps=()):
    from contextlib import ExitStack
    PH = PHASES.index(upto)
    taps = set(taps)
    nc = bass.Bass("TRN2", target_bir_lowering=False)
    P = Prog()

    x_in = nc.dram_tensor("x", [S, D], F32, kind="ExternalInput").ap()
    xown_in = nc.dram_tensor("xown", [TPC, D], F32, kind="ExternalInput").ap()
    wqkv_in = nc.dram_tensor("wqkv", [D, 3072], F32, kind="ExternalInput").ap()
    wout_in = nc.dram_tensor("wout", [D, D], F32, kind="ExternalInput").ap()
    gate_in = nc.dram_tensor("gate", [D, NE], F32, kind="ExternalInput").ap()
    gup_in = nc.dram_tensor("gup", [NE, D, 2 * DI], F32, kind="ExternalInput").ap()
    gdn_in = nc.dram_tensor("gdn", [NE, DI, D], F32, kind="ExternalInput").ap()
    ident_in = nc.dram_tensor("ident", [128, 128], BF16, kind="ExternalInput").ap()
    maskc_in = nc.dram_tensor("maskc", [128, 2048], BF16, kind="ExternalInput").ap()
    out_ext = nc.dram_tensor("out", [TPC, D], F32, kind="ExternalOutput").ap()

    tap_ext = {}
    tap_shapes = {
        "qT": ([HPC, 128, S], BF16), "kT": ([HPC, 128, S], BF16),
        "v": ([S, HPC * DH], BF16), "ctx": ([HPC * 128, S], BF16),
        "y": ([TPC, D], F32), "rw": ([TPC, NE], F32),
        "gated": ([2, DI, TPC], BF16), "acc": ([TPC, D], F32),
        "ctx2": ([16 * 128, TPC], BF16),
    }
    for t in taps:
        tap_ext[t] = nc.dram_tensor("tap_" + t, *tap_shapes[t], kind="ExternalOutput").ap()

    xbf_h = nc.dram_tensor("xbf_h", [S, D], BF16).ap()
    qT_h = nc.dram_tensor("qT_h", [HPC, 128, S], BF16).ap()
    kT_h = nc.dram_tensor("kT_h", [HPC, 128, S], BF16).ap()
    v_h = nc.dram_tensor("v_h", [S, HPC * DH], BF16).ap()
    y_h = nc.dram_tensor("y_h", [TPC, D], F32).ap()
    ybf_h = nc.dram_tensor("ybf_h", [TPC, D], BF16).ap()
    gated_h = nc.dram_tensor("gated_h", [NE, DI, TPC], BF16).ap()
    ccin = nc.dram_tensor("ccin", [HPC * 128, S // 2], F32)
    ccout = nc.dram_tensor("ccout", [2 * HPC * 128, S // 2], F32)

    with ExitStack() as ctx:
        E = ctx.enter_context
        AR = 102400
        arena = E(nc.sbuf_tensor("arena", [128, AR], BF16))
        ident = E(nc.sbuf_tensor("identt", [128, 128], BF16))
        RW = E(nc.sbuf_tensor("RWt", [128, 512], F32))
        stats = E(nc.sbuf_tensor("statst", [128, 16], F32))
        rt = E(nc.sbuf_tensor("rtt", [128, 160], F32))
        cbias = E(nc.sbuf_tensor("cbiast", [128, 2], F32))
        onec = E(nc.sbuf_tensor("onect", [128, 1], BF16))
        oner = E(nc.sbuf_tensor("onert", [1, 128], F32))
        recrow = E(nc.sbuf_tensor("recrowt", [1, 512], F32))
        psum = E(nc.psum_tensor("psumt", [128, 4096], F32))

        sems = {}
        for snm in ("pe", "dve", "act", "pool", "sd", "pd", "cc0", "cc1", "cc2", "cc3", "cc4"):
            sems[snm] = E(nc.semaphore("sem_" + snm))
        r_po = E(nc.gpsimd.register("r_po"))
        r_cc = [E(nc.gpsimd.register(f"r_cc{i}")) for i in range(8)]
        block = E(nc.Block())

        A = Arena(P, AR)
        sq_r = [None]

        def ps(i, n=1):
            return psum.ap()[:, i * 512:(i + n) * 512]

        def psb4(i):
            return psum.ap()[:, i * 512:(i + 1) * 512].bitcast(BF16)[:, 0:512]

        PSR = [f"ps{i}" for i in range(8)]

        def bf(rng):
            return arena.ap()[:, rng[0]:rng[1]]

        def f32(rng):
            return arena.ap()[:, rng[0]:rng[1]].bitcast(F32)

        def ln_chain(src_ap, src_reg, dst_ap, dst_reg, par=0):
            q = par * 8
            lr = f"lnstats{par}"
            sqscr = f32(sq_r[0])
            P.emit("dve", lambda o: o.tensor_scalar_mul(stats[:, q + 1:q + 2], stats[:, q:q + 1], -1.0 / 2048.0),
                   reads=[lr], writes=[lr], tag="lnm")
            P.emit("dve", lambda o: o.tensor_scalar_add(src_ap, src_ap, stats[:, q + 1:q + 2]),
                   reads=[lr, src_reg], writes=[src_reg], tag="lnt")
            P.emit("act", lambda o: o.activation(sqscr[:, :], src_ap, AF.Square,
                                                 accum_out=stats[:, q + 2:q + 3]),
                   reads=[src_reg], writes=[f"sqjunk{par}", lr], tag="lnsq")
            P.emit("act", lambda o: o.activation(stats[:, q + 3:q + 4], stats[:, q + 2:q + 3], AF.Ln,
                                                 scale=1.0 / 2048.0, bias=cbias[:, 1:2]),
                   reads=[lr, "cbias"], writes=[lr], tag="lnln")
            P.emit("act", lambda o: o.activation(stats[:, q + 4:q + 5], stats[:, q + 3:q + 4], AF.Exp, scale=-0.5),
                   reads=[lr], writes=[lr], tag="lnexp")
            P.emit("dve", lambda o: o.tensor_scalar_mul(dst_ap, src_ap, stats[:, q + 4:q + 5]),
                   reads=[lr, src_reg], writes=[dst_reg], tag="lny")

        P.emit("dve", lambda o: o.memset(cbias[:, 0:1], -C_EXP),
               writes=["cbias"], tag="cb0")
        P.emit("dve", lambda o: o.memset(cbias[:, 1:2], EPS),
               writes=["cbias"], tag="cb1")
        P.emit("dve", lambda o: o.memset(onec[:, :], 1.0),
               writes=["onec"], tag="cb2")
        P.emit("dve", lambda o: o.memset(oner[:, :], 1.0),
               writes=["oner"], tag="cb3")
        P.emit("sp", lambda o: o.dma_start(ident[:, :], ident_in),
               writes=["ident"], kind="d", tag="ld_ident")
        def regsetup(o):
            pid = o.partition_id()
            o.reg_mov(r_po, pid)
            o.reg_mod(r_po, r_po, 2)
            ins = o.reg_mul(r_po, r_po, 1024)
            for k in range(4):
                for rank in range(2):
                    i = k * 2 + rank
                    base = rank * 1024 * 2048 + 2 * k * 128 * 2048
                    o.reg_mov(r_cc[i], base)
                    ins = o.reg_add(r_cc[i], r_cc[i], r_po)
            return ins
        P.emit("pool", regsetup, tag="regs")

        def xcast(i):
            sl = slice(i * 512, (i + 1) * 512)
            P.emit("pool", lambda o, sl=sl: o.dma_start(xbf_h[sl, :], x_in[sl, :]),
                   writes=[f"d_xbf{i}"], kind="d", tag="xcast")
        xcast(0)
        xcast(1)

        A.mark()
        wq = A.alloc("wq", 16 * 1024)
        wk = A.alloc("wk", 16 * 1024)
        wv = A.alloc("wv", 16 * 1024)
        xT = [A.alloc(f"xT{r}", 16 * 1024) for r in range(2)]
        stage_r = A.alloc("stage", 4 * 512)
        for part, reg, rname in ((0, wq, "wq"), (1, wk, "wk"), (2, wv, "wv")):
            dst = bf(reg).rearrange("p (c n) -> p c n", n=1024)
            src = wqkv_in.rearrange("(c p) n -> p c n", p=128)[:, :, part * 1024:(part + 1) * 1024]
            for cc4 in range(4):
                csl = slice(cc4 * 4, cc4 * 4 + 4)
                P.emit("pool", lambda o, d=dst[:, csl, :], s=src[:, csl, :]: o.dma_start(d, s),
                       writes=[rname], kind="d", tag="ld_wqkv")
        for i in range(2, 8):
            xcast(i)

        def xT_ap(r, d, tsl=slice(0, 1024)):
            a = xT[r][0]
            return arena.ap()[:, a + d * 1024:a + (d + 1) * 1024][:, tsl]

        stg_n = [0]

        def stage_slot():
            i = stg_n[0] % 4
            stg_n[0] += 1
            a = stage_r[0]
            return arena.ap()[:, a + i * 512:a + (i + 1) * 512], f"stg{i}"

        if PH >= 0:
            for sti in range(4):
                r = sti % 2
                t0 = sti * 1024
                for d in range(16):
                    P.emit("sp", lambda o, d=d, r=r, t0=t0: o.dma_start(
                        xT_ap(r, d), xbf_h[t0:t0 + 1024, d * 128:(d + 1) * 128], transpose=True),
                        reads=[f"d_xbf{2 * sti}", f"d_xbf{2 * sti + 1}"],
                        writes=[f"xT{r}"], kind="d", tag="xT")
                for wname, wreg, dest, dreg, scale in (
                        ("wk", wk, kT_h, "d_k", 1.0), ("wq", wq, qT_h, "d_q", QSCALE)):
                    for h in range(HPC):
                        bk0 = (h % 2) * 2
                        for d in range(16):
                            wap = bf(wreg)[:, d * 1024 + h * 128: d * 1024 + (h + 1) * 128]
                            for th in range(2):
                                P.emit("pe", lambda o, w=wap, r=r, d=d, th=th, bk0=bk0,
                                       s=(d == 0), e=(d == 15):
                                       o.matmul(ps(bk0 + th), w,
                                                xT_ap(r, d, slice(th * 512, (th + 1) * 512)),
                                                start=s, stop=e),
                                       reads=[wname, f"xT{r}"], writes=[PSR[bk0 + th]], tag="kqproj")
                        for th in range(2):
                            sap, sreg = stage_slot()
                            P.emit("act", lambda o, sap=sap, bank=bk0 + th, sc=scale:
                                   o.activation(sap, ps(bank), AF.Copy, scale=sc),
                                   reads=[PSR[bk0 + th]], writes=[sreg], tag="kqevac")
                            P.emit("pool", lambda o, sap=sap, h=h, t1=t0 + th * 512, dst=dest:
                                   o.dma_start(dst[h, :, t1:t1 + 512], sap),
                                   reads=[sreg], writes=[dreg], kind="d", tag="kqst")
                for sub in range(8):
                    for vch in range(2):
                        bank = 4 + ((sub * 2 + vch) % 4)
                        tsl = slice(sub * 128, (sub + 1) * 128)
                        for d in range(16):
                            P.emit("pe", lambda o, r=r, d=d, tsl=tsl, vch=vch, bank=bank, s=(d == 0), e=(d == 15):
                                   o.matmul(ps(bank), xT_ap(r, d, tsl),
                                            bf(wv)[:, d * 1024 + vch * 512: d * 1024 + (vch + 1) * 512],
                                            start=s, stop=e),
                                   reads=["wv", f"xT{r}"], writes=[PSR[bank]], tag="vproj")
                        sap, sreg = stage_slot()
                        P.emit("act", lambda o, sap=sap, bank=bank:
                               o.activation(sap, ps(bank), AF.Copy),
                               reads=[PSR[bank]], writes=[sreg], tag="vevac")
                        P.emit("pool", lambda o, sap=sap, t0=t0, sub=sub, vch=vch:
                               o.dma_start(v_h[t0 + sub * 128: t0 + (sub + 1) * 128,
                                               vch * 512:(vch + 1) * 512], sap),
                               reads=[sreg], writes=["d_v"], kind="d", tag="vst")
        A.release()

        for t in ("qT", "kT", "v"):
            if t in taps:
                src = {"qT": qT_h, "kT": kT_h, "v": v_h}[t]
                dreg = {"qT": "d_q", "kT": "d_k", "v": "d_v"}[t]
                P.emit("sp", lambda o, s=src, d2=tap_ext[t]: o.dma_start(d2, s),
                       reads=[dreg], kind="d", tag="tap")

        A.mark()
        ctx_r = A.alloc("ctx", HPC * 4096)
        A.mark()
        maskc_r = A.alloc("maskc", 2048)
        P.emit("sp", lambda o: o.dma_start(bf(maskc_r), maskc_in),
               writes=["maskc"], kind="d", tag="ld_mask")
        kb_ = [A.alloc(f"kb{r}", 4096) for r in range(2)]
        vb_ = [A.alloc(f"vb{r}", 4096) for r in range(2)]
        qb_ = [A.alloc(f"qb{r}", 4096) for r in range(2)]
        PT_ = [A.alloc(f"PT{r}", 32 * 512) for r in range(2)]
        rep_ = [A.alloc(f"rep{r}", 2 * 512) for r in range(2)]

        if PH >= 1:
            for h in range(HPC):
                r = h % 2
                P.emit("sp", lambda o, h=h, r=r: o.dma_start(bf(kb_[r]), kT_h[h]),
                       reads=["d_k"], writes=[f"kb{r}"], kind="d", tag="kld")
                P.emit("sp", lambda o, h=h, r=r: o.dma_start(bf(qb_[r]), qT_h[h]),
                       reads=["d_q"], writes=[f"qb{r}"], kind="d", tag="qld")
                vsrc = v_h[:, h * 128:(h + 1) * 128].rearrange("(t p) n -> p t n", p=128)
                vdst = bf(vb_[r]).rearrange("p (t n) -> p t n", n=128)
                for q4 in range(4):
                    tsl = slice(q4 * 8, q4 * 8 + 8)
                    P.emit("sp", lambda o, s=vsrc[:, tsl, :], d2=vdst[:, tsl, :]: o.dma_start(d2, s),
                           reads=["d_v"], writes=[f"vb{r}"], kind="d", tag="vld")

                for s_ in range(NST):
                    NB = s_ + 1
                    Q0 = s_ * STW
                    cxb = 4 + (s_ % 2)
                    sp_ = s_ % 2
                    NKB = 4 * NB
                    ptr = f"PT{sp_}"
                    PT_r = PT_[sp_]

                    for kb in range(NKB):
                        bank = [0, 6][(kb // 2) % 2] + (kb % 2)
                        P.emit("pe", lambda o, kb=kb, bank=bank, r=r, Q0=Q0:
                               o.matmul(ps(bank),
                                        bf(kb_[r])[:, kb * 128:(kb + 1) * 128],
                                        bf(qb_[r])[:, Q0:Q0 + 512],
                                        start=True, stop=True),
                               reads=[f"kb{r}", f"qb{r}"], writes=[PSR[bank]], tag="sc")
                        if kb % 2 == 1:
                            b0 = [0, 6][((kb - 1) // 2) % 2]
                            P.emit("act", lambda o, b0=b0, kb=kb, ptr=ptr, PT_r=PT_r:
                                   o.activation(bf(PT_r)[:, (kb - 1) * 512:(kb + 1) * 512],
                                                ps(b0, 2), AF.Exp, bias=cbias[:, 0:1]),
                                   reads=[PSR[b0], PSR[b0 + 1], "cbias"],
                                   writes=[ptr], tag="exp2")
                    for m in range(4):
                        kb = 4 * s_ + m
                        P.emit("dve", lambda o, kb=kb, m=m, ptr=ptr, PT_r=PT_r:
                               o.tensor_mul(bf(PT_r)[:, kb * 512:(kb + 1) * 512],
                                            bf(PT_r)[:, kb * 512:(kb + 1) * 512],
                                            bf(maskc_r)[:, m * 512:(m + 1) * 512]),
                               reads=[ptr, "maskc"], writes=[ptr], tag="maskd")
                    for kb in range(NKB):
                        P.emit("pe", lambda o, kb=kb, PT_r=PT_r, s4=(kb == 0), e4=(kb == NKB - 1):
                               o.matmul(ps(2)[0:1, :], onec[:, :],
                                        bf(PT_r)[:, kb * 512:(kb + 1) * 512],
                                        start=s4, stop=e4),
                               reads=["onec", ptr], writes=[PSR[2]], tag="ones")
                    for kb in range(NKB):
                        P.emit("pe", lambda o, kb=kb, r=r, PT_r=PT_r, s4=(kb == 0), e4=(kb == NKB - 1), bank=cxb:
                               o.matmul(ps(bank),
                                        bf(vb_[r])[:, kb * 128:(kb + 1) * 128],
                                        bf(PT_r)[:, kb * 512:(kb + 1) * 512],
                                        start=s4, stop=e4),
                               reads=[f"vb{r}", ptr], writes=[PSR[cxb]], tag="av")
                    P.emit("dve", lambda o: o.reciprocal(recrow[:, :], ps(2)[0:1, :]),
                           reads=[PSR[2]], writes=["recrow"], tag="drec")
                    P.emit("pe", lambda o: o.matmul(ps(3), oner[:, :], recrow[:, :],
                                                    start=True, stop=True),
                           reads=["oner", "recrow"], writes=[PSR[3]], tag="rep")
                    P.emit("act", lambda o, sp_=sp_: o.activation(f32(rep_[sp_]), ps(3), AF.Copy),
                           reads=[PSR[3]], writes=[f"rep{sp_}"], tag="repev")
                    P.emit("dve", lambda o, h=h, Q0=Q0, sp_=sp_, bank=cxb:
                           o.tensor_mul(bf(ctx_r)[:, h * 4096 + Q0: h * 4096 + Q0 + STW],
                                        ps(bank), f32(rep_[sp_])),
                           reads=[PSR[cxb], f"rep{sp_}"], writes=[f"ctx{h}"], tag="ctxev")

                if PH >= 2 and h % 2 == 1:
                    k = h // 2
                    for hh in (h - 1, h):
                        P.emit("sp", lambda o, hh=hh: o.dma_start(
                            ccin.ap()[hh * 128:(hh + 1) * 128, :],
                            bf(ctx_r)[:, hh * 4096:(hh + 1) * 4096].bitcast(F32)),
                            reads=[f"ctx{hh}"], writes=[f"d_ccin{k}", f"ctxg{k}"],
                            kind="d", tag="ccst")
                    if k == 3:
                        P.emit("pool", lambda o: o.collective_compute(
                            "AllGather", ALU.bypass,
                            replica_groups=[[0, 1], [2, 3], [4, 5], [6, 7]],
                            ins=[ccin.ap().opt()], outs=[ccout.ap().opt()]),
                            reads=["d_ccin0", "d_ccin1", "d_ccin2", "d_ccin3"],
                            writes=["d_ccout"], kind="cc", tag="ccg")
        A.release()

        if "ctx" in taps:
            for h in range(HPC):
                P.emit("sp", lambda o, h=h: o.dma_start(
                    tap_ext["ctx"][h * 128:(h + 1) * 128, :], bf(ctx_r)[:, h * 4096:(h + 1) * 4096]),
                    reads=[f"ctx{h}"], kind="d", tag="tapctx")

        def ctx2_chunk(c):
            k = (c % 8) // 2
            pos = (c % 8) % 2 + 2 * (c // 8)
            a = ctx_r[0] + 2 * k * 4096 + pos * 2048
            return arena.ap()[:, a:a + 2048]

        if PH >= 2:
            for k in range(4):
                a = ctx_r[0] + 2 * k * 4096
                dstk = arena.ap()[:, a:a + 4 * 2048].bitcast(F32).rearrange(
                    "p (c n) -> p c n", n=1024)
                for rank in range(2):
                    src = bass.AP(ccout, r_cc[k * 2 + rank],
                                  [[2048, 128], [128 * 2048, 2], [1, 1024]])
                    P.emit("pool", lambda o, d=dstk[:, rank * 2:rank * 2 + 2, :], s=src:
                           o.dma_start(d, s),
                           reads=["d_ccout"],
                           writes=[f"ctxg{k}"], kind="d", tag="ccld")
        if "ctx2" in taps:
            for c in range(16):
                P.emit("sp", lambda o, c=c: o.dma_start(
                    tap_ext["ctx2"][c * 128:(c + 1) * 128, :], ctx2_chunk(c)),
                    reads=[f"ctxg{(c % 8) // 2}"], kind="d", tag="tapctx2")

        if PH >= 3:
            sq_r[0] = A.alloc("sqscrA", 2 * 2048)
            wout_r = A.alloc("wout", 16 * 2048)
            xs_ = [A.alloc(f"xs{r}", 2 * 2048) for r in range(2)]
            h1_ = [A.alloc(f"h1{r}", 2 * 2048) for r in range(2)]
            y_ = [A.alloc(f"yb{r}", 2 * 2048) for r in range(2)]
            dstw = bf(wout_r).rearrange("p (c n) -> p c n", n=2048)
            srcw = wout_in.rearrange("(c p) n -> p c n", p=128)
            for cc4 in range(4):
                csl = slice(cc4 * 4, cc4 * 4 + 4)
                P.emit("pool", lambda o, d=dstw[:, csl, :], s=srcw[:, csl, :]: o.dma_start(d, s),
                       writes=["wout"], kind="d", tag="ld_wout")
            for tt in range(16):
                r = tt % 2
                grp = 0 if r == 0 else 4
                banks = [PSR[grp + i] for i in range(4)]
                P.emit("sp", lambda o, tt=tt, r=r: o.dma_start(
                    f32(xs_[r]), xown_in[tt * 128:(tt + 1) * 128, :]),
                    writes=[f"xs{r}"], kind="d", tag="xld")
                for d in range(16):
                    for ct in range(4):
                        P.emit("pe", lambda o, d=d, ct=ct, tt=tt, grp=grp, s=(d == 0), e=(d == 15):
                               o.matmul(ps(grp + ct),
                                        ctx2_chunk(d)[:, tt * 128:(tt + 1) * 128],
                                        bf(wout_r)[:, d * 2048 + ct * 512: d * 2048 + (ct + 1) * 512],
                                        start=s, stop=e),
                               reads=[f"ctxg{(d % 8) // 2}", "wout"], writes=[banks[ct]], tag="oproj")
                P.emit("dve", lambda o, r=r, grp=grp:
                       o.scalar_tensor_tensor(f32(h1_[r]), ps(grp, 4), 1.0, f32(xs_[r]),
                                              ALU.mult, ALU.add, accum_out=stats[:, r * 8:r * 8 + 1]),
                       reads=banks + [f"xs{r}"], writes=[f"h1{r}", f"lnstats{r}"], tag="h1")
                ln_chain(f32(h1_[r]), f"h1{r}", f32(y_[r]), f"yb{r}", par=r)
                P.emit("sp", lambda o, tt=tt, r=r: o.dma_start(
                    y_h[tt * 128:(tt + 1) * 128, :], f32(y_[r])),
                    reads=[f"yb{r}"], writes=["d_y"], kind="d", tag="yst")
                P.emit("pool", lambda o, tt=tt, r=r: o.dma_start(
                    ybf_h[tt * 128:(tt + 1) * 128, :], f32(y_[r])),
                    reads=[f"yb{r}"], writes=["d_ybf"], kind="d", tag="ybfst")
        A.release()

        if "y" in taps:
            P.emit("sp", lambda o: o.dma_start(tap_ext["y"], y_h),
                   reads=["d_y"], kind="d", tag="tapy")

        A.mark()
        yT_r = A.alloc("yT", 16 * 2048)
        if PH >= 4:
            gateb_r = A.alloc("gateb", 16 * 32)
            for d in range(16):
                P.emit("sp", lambda o, d=d: o.dma_start(
                    bf(yT_r)[:, d * 2048:(d + 1) * 2048],
                    ybf_h[:, d * 128:(d + 1) * 128], transpose=True),
                    reads=["d_ybf"], writes=["yT"], kind="d", tag="yT")
            P.emit("pool", lambda o: o.dma_start(
                bf(gateb_r).rearrange("p (c n) -> p c n", n=32),
                gate_in.rearrange("(c p) n -> p c n", p=128)),
                writes=["gateb"], kind="d", tag="ld_gate")
            for tt in range(16):
                bank = 6 + tt % 2
                for d in range(16):
                    P.emit("pe", lambda o, d=d, tt=tt, bank=bank, s=(d == 0), e=(d == 15):
                           o.matmul(ps(bank)[:, 0:32],
                                    bf(yT_r)[:, d * 2048 + tt * 128: d * 2048 + (tt + 1) * 128],
                                    bf(gateb_r)[:, d * 32:(d + 1) * 32],
                                    start=s, stop=e),
                           reads=["yT", "gateb"], writes=[PSR[bank]], tag="rlog")
                lg = rt[:, 0:32]
                rp = rt[:, 32:64]
                t8 = rt[:, 64:72]
                zap = rt[:, 96:128]
                P.emit("dve", lambda o, bank=bank, lg=lg: o.tensor_copy(lg, ps(bank)[:, 0:32]),
                       reads=[PSR[bank]], writes=["rt"], tag="rcp")
                P.emit("dve", lambda o, lg=lg: o.reduce_max(rt[:, 72:73], lg, axis=AX.X, negate=True),
                       reads=["rt"], writes=["rt"], tag="rmax")
                P.emit("act", lambda o, lg=lg, rp=rp: o.activation(
                    rp, lg, AF.Exp, bias=rt[:, 72:73], accum_out=rt[:, 73:74]),
                    reads=["rt"], writes=["rt"], tag="rexp")
                P.emit("dve", lambda o: o.reciprocal(rt[:, 74:75], rt[:, 73:74]),
                       reads=["rt"], writes=["rt"], tag="rrec")
                P.emit("dve", lambda o, rp=rp: o.tensor_scalar_mul(rp, rp, rt[:, 74:75]),
                       reads=["rt"], writes=["rt"], tag="rnorm")
                P.emit("dve", lambda o, rp=rp, t8=t8: o.max(out=t8, in_=rp),
                       reads=["rt"], writes=["rt"], tag="rtop8")
                P.emit("dve", lambda o: o.memset(rt[:, 68:72], -1e30),
                       reads=["rt"], writes=["rt"], tag="rmemset")
                P.emit("dve", lambda o, rp=rp, t8=t8, zap=zap: o.match_replace(
                    out=zap, in_to_replace=t8, in_values=rp, imm_value=0.0),
                    reads=["rt"], writes=["rt"], tag="rzap")
                P.emit("dve", lambda o, rp=rp, zap=zap: o.tensor_sub(rt[:, 128:160], rp, zap),
                       reads=["rt"], writes=["rt"], tag="rsub")
                P.emit("dve", lambda o: o.reduce_sum(rt[:, 75:76], rt[:, 64:68], axis=AX.X),
                       reads=["rt"], writes=["rt"], tag="rsum4")
                P.emit("dve", lambda o: o.reciprocal(rt[:, 76:77], rt[:, 75:76]),
                       reads=["rt"], writes=["rt"], tag="rrec4")
                P.emit("dve", lambda o, tt=tt: o.tensor_scalar_mul(
                    RW[:, tt * 32:(tt + 1) * 32], rt[:, 128:160], rt[:, 76:77]),
                    reads=["rt"], writes=["RW"], tag="rwst")
        if "rw" in taps:
            P.emit("sp", lambda o: o.dma_start(
                tap_ext["rw"].rearrange("(t p) n -> p t n", p=128),
                RW.ap().rearrange("p (t n) -> p t n", n=32)),
                reads=["RW"], kind="d", tag="taprw")

        if PH >= 5:
            wguA = A.alloc("wguA", 16 * 768)
            wguB = [A.alloc(f"wguB{r}", 16 * 768) for r in range(2)]
            sg_r = A.alloc("sg", 6 * 2048)
            gt_r = A.alloc("gt", 6 * 2048)

            def wgu_load(e, half):
                srcw2 = gup_in[e].rearrange("(c p) n -> p c n", p=128)
                reg, rname = (wguA, "wguA") if half == 0 else (wguB[e % 2], f"wguB{e % 2}")
                dstw2 = bf(reg).rearrange("p (c n) -> p c n", n=768)
                for q in range(2):
                    csl = slice(q * 8, q * 8 + 8)
                    P.emit("pool", lambda o, d=dstw2[:, csl, :],
                           s=srcw2[:, csl, half * 768:(half + 1) * 768]:
                           o.dma_start(d, s),
                           writes=[rname], kind="d", tag="ld_wgu")
            wgu_load(0, 0)
            wgu_load(0, 1)
            wgu_load(1, 1)
            for e in range(NE):
                for gci, gc in enumerate(list(range(6)) + list(range(6, 12))):
                    bank0 = (gc % 2) * 4
                    wreg, wr = (wguA, "wguA") if gc < 6 else (wguB[e % 2], f"wguB{e % 2}")
                    gl_ = gc if gc < 6 else gc - 6
                    for d in range(16):
                        for mt in range(4):
                            P.emit("pe", lambda o, gl_=gl_, d=d, mt=mt, wreg=wreg, bank0=bank0,
                                   s=(d == 0), e2=(d == 15):
                                   o.matmul(ps(bank0 + mt),
                                            bf(wreg)[:, d * 768 + gl_ * 128: d * 768 + (gl_ + 1) * 128],
                                            bf(yT_r)[:, d * 2048 + mt * 512: d * 2048 + (mt + 1) * 512],
                                            start=s, stop=e2),
                                   reads=[wr, "yT"], writes=[PSR[bank0 + mt]], tag="gu")
                    if gci == 5 and e + 1 < NE:
                        wgu_load(e + 1, 0)
                    if gci == 11 and e + 2 < NE:
                        wgu_load(e + 2, 1)
                    if gc < 6:
                        for mt in range(4):
                            P.emit("act", lambda o, gc=gc, mt=mt, bank0=bank0:
                                   o.activation(bf(sg_r)[:, gc * 2048 + mt * 512: gc * 2048 + (mt + 1) * 512],
                                                ps(bank0 + mt), AF.Silu),
                                   reads=[PSR[bank0 + mt]], writes=["sg"], tag="silu")
                    else:
                        i = gc - 6
                        for mt in range(4):
                            P.emit("dve", lambda o, i=i, mt=mt, bank0=bank0:
                                   o.tensor_mul(bf(gt_r)[:, i * 2048 + mt * 512: i * 2048 + (mt + 1) * 512],
                                                bf(sg_r)[:, i * 2048 + mt * 512: i * 2048 + (mt + 1) * 512],
                                                ps(bank0 + mt)),
                                   reads=[PSR[bank0 + mt], "sg"],
                                   writes=[f"gt{i}"], tag="gmul")
                        P.emit("sp", lambda o, e=e, i=i:
                               o.dma_start(gated_h[e][i * 128:(i + 1) * 128, :],
                                           bf(gt_r)[:, i * 2048:(i + 1) * 2048]),
                               reads=[f"gt{i}"], writes=[f"d_g{e}"], kind="d", tag="gst")
        A.release()
        if "gated" in taps:
            for e in range(2):
                P.emit("sp", lambda o, e=e: o.dma_start(tap_ext["gated"][e], gated_h[e]),
                       reads=[f"d_g{e}"], kind="d", tag="tapg")

        A.mark()
        acc_r = A.alloc("acc", 16 * 2 * 2048)

        def acc_ap(tt, csl=slice(0, 2048)):
            a = acc_r[0]
            return arena.ap()[:, a + tt * 4096: a + (tt + 1) * 4096].bitcast(F32)[:, csl]

        if PH >= 6:
            A.mark()
            gl_ = [A.alloc(f"gl{r}", 3 * 2048) for r in range(2)]
            wd_ = [A.alloc(f"wd{r}", 6 * 2048) for r in range(2)]
            for tt in range(16):
                P.emit("dve", lambda o, tt=tt: o.memset(acc_ap(tt), 0.0),
                       writes=[f"acc{tt}"], tag="accz")

            def wd_load(e):
                slot = e % 2
                wsrc = gdn_in[e].rearrange("(c p) n -> p c n", p=128)
                for half in range(2):
                    P.emit("pool", lambda o, slot=slot, s=wsrc[:, half * 3:half * 3 + 3, :],
                           half=half:
                           o.dma_start(bf(wd_[slot]).rearrange("p (c n) -> p c n", n=2048)[:, half * 3:half * 3 + 3, :], s),
                           writes=[f"wd{slot}"], kind="d", tag="ld_wd")
            wd_load(0)
            for e in range(NE):
                for half in range(2):
                    gsrc = gated_h[e].rearrange("(c p) n -> p c n", p=128)
                    P.emit("sp", lambda o, slot=half, s=gsrc[:, half * 3:half * 3 + 3, :]:
                           o.dma_start(bf(gl_[slot]).rearrange("p (c n) -> p c n", n=2048), s),
                           reads=[f"d_g{e}"], writes=[f"gl{half}"], kind="d", tag="gld")
                if e + 1 < NE:
                    wd_load(e + 1)
                for tt in range(16):
                    grp = (tt % 2) * 4
                    for i in range(6):
                        slot = i // 3
                        il = i % 3
                        for ct in range(4):
                            P.emit("pe", lambda o, i=i, il=il, ct=ct, tt=tt, slot=slot, grp=grp,
                                   ws=e % 2, s=(i == 0), e2=(i == 5):
                                   o.matmul(ps(grp + ct),
                                            bf(gl_[slot])[:, il * 2048 + tt * 128: il * 2048 + (tt + 1) * 128],
                                            bf(wd_[ws])[:, i * 2048 + ct * 512: i * 2048 + (ct + 1) * 512],
                                            start=s, stop=e2),
                                   reads=[f"gl{slot}", f"wd{e % 2}"], writes=[PSR[grp + ct]], tag="dn")
                    for ct in range(4):
                        P.emit("dve", lambda o, tt=tt, ct=ct, e=e, grp=grp:
                               o.scalar_tensor_tensor(
                                   acc_ap(tt, slice(ct * 512, (ct + 1) * 512)),
                                   ps(grp + ct), RW[:, tt * 32 + e: tt * 32 + e + 1],
                                   acc_ap(tt, slice(ct * 512, (ct + 1) * 512)),
                                   ALU.mult, ALU.add),
                               reads=[PSR[grp + ct], "RW", f"acc{tt}"], writes=[f"acc{tt}"], tag="dacc")
            A.release()
        if "acc" in taps:
            for tt in range(16):
                P.emit("sp", lambda o, tt=tt: o.dma_start(
                    tap_ext["acc"][tt * 128:(tt + 1) * 128, :], acc_ap(tt)),
                    reads=[f"acc{tt}"], kind="d", tag="tapacc")

        if PH >= 7:
            sq_r[0] = A.alloc("sqscrB", 2 * 2048)
            yl_ = [A.alloc(f"yl{r}", 2 * 2048) for r in range(2)]
            h2_ = [A.alloc(f"h2{r}", 2 * 2048) for r in range(2)]
            ob_ = [A.alloc(f"ob{r}", 2 * 2048) for r in range(2)]
            for tt in range(16):
                r = tt % 2
                P.emit("sp", lambda o, tt=tt, r=r: o.dma_start(
                    f32(yl_[r]), y_h[tt * 128:(tt + 1) * 128, :]),
                    reads=["d_y"], writes=[f"yl{r}"], kind="d", tag="yld")
                P.emit("dve", lambda o, tt=tt, r=r:
                       o.scalar_tensor_tensor(f32(h2_[r]), acc_ap(tt), 1.0,
                                              f32(yl_[r]), ALU.mult, ALU.add,
                                              accum_out=stats[:, r * 8:r * 8 + 1]),
                       reads=[f"acc{tt}", f"yl{r}"], writes=[f"h2{r}", f"lnstats{r}"], tag="h2")
                ln_chain(f32(h2_[r]), f"h2{r}", f32(ob_[r]), f"ob{r}", par=r)
                P.emit("sp", lambda o, tt=tt, r=r: o.dma_start(
                    out_ext[tt * 128:(tt + 1) * 128, :], f32(ob_[r])),
                    reads=[f"ob{r}"], writes=["d_out"], kind="d", tag="outst")
        A.release()

        totals = P.finalize()

        def replay(e):
            def go(engobj):
                for op in P.streams[e]:
                    for (snm, val) in op.waits:
                        engobj.wait_ge(sems[snm], val)
                    ins = op.fn(engobj)
                    if op.needs_inc and ins is not None:
                        ins.then_inc(sems[op.inc_sem], 16 if op.kind == "d" else 1)
                if e == "sp":
                    engobj.wait_ge(sems["sd"], totals["sd"])
                    if totals["pd"]:
                        engobj.wait_ge(sems["pd"], totals["pd"])
            return go

        block.tensor(replay("pe"))
        block.vector(replay("dve"))
        block.scalar(replay("act"))
        block.gpsimd(replay("pool"))
        block.sync(replay("sp"))

    return nc


_CACHE = {}


def make_in_maps(L0_attn_x, L0_attn_w_qkv, L0_attn_w_out, L0_ffn_gate,
                 L0_ffn_gate_up_proj, L0_ffn_down_proj):
    x = np.ascontiguousarray(np.asarray(L0_attn_x, dtype=np.float32))
    wqkv = np.asarray(L0_attn_w_qkv, dtype=np.float32)
    wout = np.ascontiguousarray(np.asarray(L0_attn_w_out, dtype=np.float32))
    gate = np.ascontiguousarray(np.asarray(L0_ffn_gate, dtype=np.float32))
    gup = np.ascontiguousarray(np.asarray(L0_ffn_gate_up_proj, dtype=np.float32))
    gdn = np.ascontiguousarray(np.asarray(L0_ffn_down_proj, dtype=np.float32))
    ident, maskc = _host_consts()
    in_maps = []
    for c in range(8):
        b, p = c // 2, c % 2
        wq = wqkv[:, p * 1024:(p + 1) * 1024]
        wk = wqkv[:, 2048 + p * 1024: 2048 + (p + 1) * 1024]
        wv = wqkv[:, 4096 + p * 1024: 4096 + (p + 1) * 1024]
        in_maps.append({
            "x": np.ascontiguousarray(x[b]),
            "xown": np.ascontiguousarray(x[b, p * 2048:(p + 1) * 2048]),
            "wqkv": np.ascontiguousarray(np.concatenate([wq, wk, wv], axis=1)),
            "wout": wout,
            "gate": gate,
            "gup": gup,
            "gdn": gdn,
            "ident": ident,
            "maskc": maskc,
        })
    return in_maps


def kernel(L0_attn_x, L0_attn_w_qkv, L0_attn_w_out, L0_ffn_gate,
           L0_ffn_gate_up_proj, L0_ffn_down_proj):
    if "nc" not in _CACHE:
        _CACHE["nc"] = build_program()
    nc = _CACHE["nc"]
    in_maps = make_in_maps(L0_attn_x, L0_attn_w_qkv, L0_attn_w_out, L0_ffn_gate,
                           L0_ffn_gate_up_proj, L0_ffn_down_proj)
    res = run_bass_kernel_spmd(nc, in_maps, core_ids=list(range(8)))
    out = np.empty((B, S, D), dtype=np.float32)
    for c in range(8):
        b, p = c // 2, c % 2
        out[b, p * 2048:(p + 1) * 2048] = res.results[c]["out"]
    return out


# version 07 (final)
# speedup vs baseline: 1.3546x; 1.1425x over previous
import os
import numpy as np
import ml_dtypes
import concourse.bass as bass
import concourse.mybir as mybir
from concourse.bass_utils import run_bass_kernel_spmd

F32 = mybir.dt.float32
BF16 = mybir.dt.bfloat16
AF = mybir.ActivationFunctionType
ALU = mybir.AluOpType
AX = mybir.AxisListType

B, S, D = 4, 4096, 2048
HPC, DH = 8, 128
TPC = 2048
NE, DI = 32, 768
STW = 512
NST = S // STW
C_EXP = 8.0
EPS = 1e-5
QSCALE = 1.0 / float(np.sqrt(np.float32(128.0)))
NEG = -1e10

ENGS = ("pe", "dve", "act", "pool", "sp")
PHASES = ["PA", "PB", "PC", "PD", "PE", "F1", "F2", "PG"]


class Op:
    __slots__ = ("eng", "fn", "reads", "writes", "kind", "idx", "eseq",
                 "waits", "needs_inc", "inc_sem", "inc_amt", "tag")

    def __init__(self, eng, fn, reads, writes, kind, idx, eseq, tag):
        self.eng, self.fn = eng, fn
        self.reads, self.writes = reads, writes
        self.kind = kind
        self.idx, self.eseq = idx, eseq
        self.waits = []
        self.needs_inc = False
        self.inc_sem = None
        self.inc_amt = 0
        self.tag = tag


class Prog:

    def __init__(self):
        self.ops = []
        self.streams = {e: [] for e in ENGS}
        self.regs = {}
        self.clock = {e: {} for e in ENGS}
        self.last_comp = {e: None for e in ENGS}
        self.dma_count = {"sp": 0, "pool": 0}
        self.cc_count = 0

    def region(self, r):
        st = self.regs.get(r)
        if st is None:
            st = [None, []]
            self.regs[r] = st
        return st

    def new_region(self, r, preds=()):
        self.regs[r] = [None, [p for p in preds if p is not None]]

    def _edge(self, op, prod):
        if prod is op:
            return
        e, f = op.eng, prod.eng
        if e == "pe" and f == "pe":
            return
        if self.clock[e].get(f, -1) >= prod.eseq:
            return
        if prod.kind in ("d", "cc"):
            op.waits.append((prod.inc_sem, prod.inc_amt))
            self.clock[e][f] = prod.eseq
            return
        if e == f:
            sub = self.last_comp[e]
            assert sub is not None and sub.eseq >= prod.eseq
            sub.needs_inc = True
            op.waits.append(sub)
            self.clock[e][f] = sub.eseq
        else:
            prod.needs_inc = True
            op.waits.append(prod)
            self.clock[e][f] = prod.eseq

    def emit(self, eng, fn, reads=(), writes=(), kind="c", tag=""):
        idx = len(self.ops)
        op = Op(eng, fn, list(reads), list(writes), kind, idx,
                len(self.streams[eng]), tag)
        self.ops.append(op)
        self.streams[eng].append(op)
        if kind == "d":
            self.dma_count[eng] += 1
            op.inc_sem = "sd" if eng == "sp" else "pd"
            op.inc_amt = 16 * self.dma_count[eng]
            op.needs_inc = True
        elif kind == "cc":
            op.inc_sem = f"cc{self.cc_count}"
            self.cc_count += 1
            op.inc_amt = 1
            op.needs_inc = True
        prods = {}
        for r in op.reads:
            st = self.region(r)
            if st[0] is not None:
                prods[st[0].idx] = st[0]
        for r in op.writes:
            st = self.region(r)
            if st[0] is not None:
                prods[st[0].idx] = st[0]
            for rd in st[1]:
                prods[rd.idx] = rd
        for pidx in sorted(prods):
            self._edge(op, prods[pidx])
        for r in op.reads:
            self.region(r)[1].append(op)
        for r in op.writes:
            st = self.region(r)
            st[0] = op
            st[1] = []
        if kind == "c":
            self.last_comp[eng] = op
        return op

    def finalize(self):
        inc_no = {}
        totals = {}
        for e in ENGS:
            n = 0
            for op in self.streams[e]:
                if op.kind == "c" and op.needs_inc:
                    n += 1
                    op.inc_sem = e
                    inc_no[op.idx] = n
            totals[e] = n
        for op in self.ops:
            byname = {}
            for w in op.waits:
                if isinstance(w, tuple):
                    s, v = w
                else:
                    s, v = w.eng, inc_no[w.idx]
                if byname.get(s, 0) < v:
                    byname[s] = v
            op.waits = sorted(byname.items())
        totals["sd"] = 16 * self.dma_count["sp"]
        totals["pd"] = 16 * self.dma_count["pool"]
        for i in range(self.cc_count):
            totals[f"cc{i}"] = 1
        for op in self.ops:
            for s, v in op.waits:
                assert v <= totals.get(s, 0), f"wait {s}>={v} exceeds total {totals.get(s)} ({op.tag})"
        return totals


class Arena:
    def __init__(self, prog, size):
        self.P = prog
        self.size = size
        self.cur = 0
        self.live = []
        self.marks = []
        self.freed = []

    def alloc(self, name, n):
        a, b = self.cur, self.cur + n
        assert b <= self.size, f"arena overflow: {name}: {b} > {self.size}"
        self.cur = b
        self.live.append((name, a, b))
        preds = []
        st_old = self.P.regs.get(name)
        if st_old is not None:
            preds += ([st_old[0]] if st_old[0] else []) + st_old[1]
        for (nm, fa, fb, fp) in self.freed:
            if fa < b and a < fb:
                preds += fp
        self.P.new_region(name, preds)
        return (a, b)

    def mark(self):
        self.marks.append((self.cur, list(self.live)))

    def release(self):
        cur0, live0 = self.marks.pop()
        for (nm, a, b) in self.live:
            if (nm, a, b) not in live0:
                preds = []
                for rn, st in self.P.regs.items():
                    if rn == nm or rn.startswith(nm):
                        preds += ([st[0]] if st[0] else []) + st[1]
                self.freed.append((nm, a, b, preds))
        self.cur = cur0
        self.live = live0


def _host_consts():
    identf = np.eye(128, dtype=np.float32)
    ident = np.eye(128, dtype=ml_dtypes.bfloat16)
    mask = np.zeros((128, 4 * 512), dtype=np.float32)
    for m in range(4):
        i = np.arange(128)[:, None]
        q = np.arange(512)[None, :]
        mask[:, m * 512:(m + 1) * 512] = (q >= m * 128 + i).astype(np.float32)
    return ident, identf, mask.astype(ml_dtypes.bfloat16)


def build_program(upto="PG", taps=()):
    from contextlib import ExitStack
    PH = PHASES.index(upto)
    taps = set(taps)
    nc = bass.Bass("TRN2", target_bir_lowering=False)
    P = Prog()

    x_in = nc.dram_tensor("x", [S, D], F32, kind="ExternalInput").ap()
    xown_in = nc.dram_tensor("xown", [TPC, D], F32, kind="ExternalInput").ap()
    wqkv_in = nc.dram_tensor("wqkv", [D, 3072], F32, kind="ExternalInput").ap()
    wout_in = nc.dram_tensor("wout", [D, D], F32, kind="ExternalInput").ap()
    gate_in = nc.dram_tensor("gate", [D, NE], F32, kind="ExternalInput").ap()
    gup_in = nc.dram_tensor("gup", [NE, D, 2 * DI], F32, kind="ExternalInput").ap()
    gdn_in = nc.dram_tensor("gdn", [NE, DI, D], F32, kind="ExternalInput").ap()
    ident_in = nc.dram_tensor("ident", [128, 128], BF16, kind="ExternalInput").ap()
    identf_in = nc.dram_tensor("identf", [128, 128], F32, kind="ExternalInput").ap()
    maskc_in = nc.dram_tensor("maskc", [128, 2048], BF16, kind="ExternalInput").ap()
    out_ext = nc.dram_tensor("out", [TPC, D], F32, kind="ExternalOutput").ap()

    tap_ext = {}
    tap_shapes = {
        "qT": ([HPC, 128, S], BF16), "kT": ([HPC, 128, S], BF16),
        "v": ([S, HPC * DH], BF16), "ctx": ([HPC * 128, S], BF16),
        "y": ([TPC, D], F32), "rw": ([TPC, NE], F32),
        "gated": ([2, DI, TPC], BF16), "acc": ([TPC, D], F32),
        "ctx2": ([16 * 128, TPC], BF16),
    }
    for t in taps:
        tap_ext[t] = nc.dram_tensor("tap_" + t, *tap_shapes[t], kind="ExternalOutput").ap()

    qT_h = nc.dram_tensor("qT_h", [HPC, 128, S], BF16).ap()
    kT_h = nc.dram_tensor("kT_h", [HPC, 128, S], BF16).ap()
    v_h = nc.dram_tensor("v_h", [S, HPC * DH], BF16).ap()
    y_h = nc.dram_tensor("y_h", [TPC, D], F32).ap()
    yT_hbm = nc.dram_tensor("yT_hbm", [D, TPC], BF16).ap()
    gated_h = nc.dram_tensor("gated_h", [NE, DI, TPC], BF16).ap()
    ccin = nc.dram_tensor("ccin", [HPC * 128, S // 2], F32)
    ccout = nc.dram_tensor("ccout", [2 * HPC * 128, S // 2], F32)

    with ExitStack() as ctx:
        E = ctx.enter_context
        AR = 102400
        arena = E(nc.sbuf_tensor("arena", [128, AR], BF16))
        ident = E(nc.sbuf_tensor("identt", [128, 128], BF16))
        RW = E(nc.sbuf_tensor("RWt", [128, 512], F32))
        stats = E(nc.sbuf_tensor("statst", [128, 16], F32))
        rt = E(nc.sbuf_tensor("rtt", [128, 160], F32))
        cbias = E(nc.sbuf_tensor("cbiast", [128, 2], F32))
        identf = E(nc.sbuf_tensor("identft", [128, 128], F32))
        onec = E(nc.sbuf_tensor("onect", [128, 1], BF16))
        oner = E(nc.sbuf_tensor("onert", [1, 128], F32))
        recrow = E(nc.sbuf_tensor("recrowt", [1, 512], F32))
        psum = E(nc.psum_tensor("psumt", [128, 4096], F32))

        sems = {}
        for snm in ("pe", "dve", "act", "pool", "sd", "pd", "cc0", "cc1", "cc2", "cc3", "cc4"):
            sems[snm] = E(nc.semaphore("sem_" + snm))
        r_po = E(nc.gpsimd.register("r_po"))
        r_cc = [E(nc.gpsimd.register(f"r_cc{i}")) for i in range(8)]
        block = E(nc.Block())

        A = Arena(P, AR)
        sq_r = [None]

        def ps(i, n=1):
            return psum.ap()[:, i * 512:(i + n) * 512]

        def psb4(i):
            return psum.ap()[:, i * 512:(i + 1) * 512].bitcast(BF16)[:, 0:512]

        PSR = [f"ps{i}" for i in range(8)]

        def bf(rng):
            return arena.ap()[:, rng[0]:rng[1]]

        def f32(rng):
            return arena.ap()[:, rng[0]:rng[1]].bitcast(F32)

        def ln_chain(src_ap, src_reg, dst_ap, dst_reg, par=0):
            q = par * 8
            lr = f"lnstats{par}"
            sqscr = f32(sq_r[0])
            P.emit("dve", lambda o: o.tensor_scalar_mul(stats[:, q + 1:q + 2], stats[:, q:q + 1], -1.0 / 2048.0),
                   reads=[lr], writes=[lr], tag="lnm")
            P.emit("dve", lambda o: o.tensor_scalar_add(src_ap, src_ap, stats[:, q + 1:q + 2]),
                   reads=[lr, src_reg], writes=[src_reg], tag="lnt")
            P.emit("act", lambda o: o.activation(sqscr[:, :], src_ap, AF.Square,
                                                 accum_out=stats[:, q + 2:q + 3]),
                   reads=[src_reg], writes=[f"sqjunk{par}", lr], tag="lnsq")
            P.emit("act", lambda o: o.activation(stats[:, q + 3:q + 4], stats[:, q + 2:q + 3], AF.Ln,
                                                 scale=1.0 / 2048.0, bias=cbias[:, 1:2]),
                   reads=[lr, "cbias"], writes=[lr], tag="lnln")
            P.emit("act", lambda o: o.activation(stats[:, q + 4:q + 5], stats[:, q + 3:q + 4], AF.Exp, scale=-0.5),
                   reads=[lr], writes=[lr], tag="lnexp")
            P.emit("dve", lambda o: o.tensor_scalar_mul(dst_ap, src_ap, stats[:, q + 4:q + 5]),
                   reads=[lr, src_reg], writes=[dst_reg], tag="lny")

        P.emit("dve", lambda o: o.memset(cbias[:, 0:1], -C_EXP),
               writes=["cbias"], tag="cb0")
        P.emit("dve", lambda o: o.memset(cbias[:, 1:2], EPS),
               writes=["cbias"], tag="cb1")
        P.emit("dve", lambda o: o.memset(onec[:, :], 1.0),
               writes=["onec"], tag="cb2")
        P.emit("dve", lambda o: o.memset(oner[:, :], 1.0),
               writes=["oner"], tag="cb3")
        P.emit("sp", lambda o: o.dma_start(ident[:, :], ident_in),
               writes=["ident"], kind="d", tag="ld_ident")
        P.emit("sp", lambda o: o.dma_start(identf[:, :], identf_in),
               writes=["identf"], kind="d", tag="ld_identf")
        def regsetup(o):
            pid = o.partition_id()
            o.reg_mov(r_po, pid)
            o.reg_mod(r_po, r_po, 2)
            ins = o.reg_mul(r_po, r_po, 1024)
            for k in range(4):
                for rank in range(2):
                    i = k * 2 + rank
                    base = rank * 1024 * 2048 + 2 * k * 128 * 2048
                    o.reg_mov(r_cc[i], base)
                    ins = o.reg_add(r_cc[i], r_cc[i], r_po)
            return ins
        P.emit("pool", regsetup, tag="regs")

        A.mark()
        wq = A.alloc("wq", 16 * 1024)
        wk = A.alloc("wk", 16 * 1024)
        wv = A.alloc("wv", 16 * 1024)
        xT = [A.alloc(f"xT{r}", 16 * 1024) for r in range(2)]
        xstg = [A.alloc(f"xstg{r}", 2 * 2048) for r in range(2)]
        stage_r = A.alloc("stage", 4 * 512)
        for part, reg, rname in ((0, wq, "wq"), (1, wk, "wk"), (2, wv, "wv")):
            dst = bf(reg).rearrange("p (c n) -> p c n", n=1024)
            src = wqkv_in.rearrange("(c p) n -> p c n", p=128)[:, :, part * 1024:(part + 1) * 1024]
            for cc4 in range(4):
                csl = slice(cc4 * 4, cc4 * 4 + 4)
                P.emit("pool", lambda o, d=dst[:, csl, :], s=src[:, csl, :]: o.dma_start(d, s),
                       writes=[rname], kind="d", tag="ld_wqkv")

        def xT_ap(r, d, tsl=slice(0, 1024)):
            a = xT[r][0]
            return arena.ap()[:, a + d * 1024:a + (d + 1) * 1024][:, tsl]

        stg_n = [0]

        def stage_slot():
            i = stg_n[0] % 4
            stg_n[0] += 1
            a = stage_r[0]
            return arena.ap()[:, a + i * 512:a + (i + 1) * 512], f"stg{i}"

        if PH >= 0:
            for sti in range(4):
                r = sti % 2
                t0 = sti * 1024
                for sub in range(8):
                    xr = sub % 2
                    P.emit("sp", lambda o, t0=t0, sub=sub, xr=xr: o.dma_start(
                        f32(xstg[xr]), x_in[t0 + sub * 128: t0 + (sub + 1) * 128, :]),
                        writes=[f"xstg{xr}"], kind="d", tag="xld")
                    for dq in range(4):
                        bank = 6 + (dq % 2)
                        for ii in range(4):
                            d = dq * 4 + ii
                            P.emit("pe", lambda o, d=d, ii=ii, xr=xr, bank=bank:
                                   o.transpose(ps(bank)[:, ii * 128:(ii + 1) * 128],
                                               f32(xstg[xr])[:, d * 128:(d + 1) * 128],
                                               identf[:, :]),
                                   reads=[f"xstg{xr}", "identf"], writes=[PSR[bank]], tag="xtr")
                        dst = bf(xT[r]).rearrange("p (c n) -> p c n", n=1024)
                        P.emit("dve", lambda o, dq=dq, sub=sub, r=r, bank=bank, dst=dst:
                               o.tensor_copy(dst[:, dq * 4:(dq + 1) * 4, sub * 128:(sub + 1) * 128],
                                             ps(bank).rearrange("p (c n) -> p c n", n=128)),
                               reads=[PSR[bank]], writes=[f"xT{r}"], tag="xtrev")
                for wname, wreg, dest, dreg, scale in (
                        ("wk", wk, kT_h, "d_k", 1.0), ("wq", wq, qT_h, "d_q", QSCALE)):
                    for h in range(HPC):
                        bk0 = (h % 2) * 2
                        for d in range(16):
                            wap = bf(wreg)[:, d * 1024 + h * 128: d * 1024 + (h + 1) * 128]
                            for th in range(2):
                                P.emit("pe", lambda o, w=wap, r=r, d=d, th=th, bk0=bk0,
                                       s=(d == 0), e=(d == 15):
                                       o.matmul(ps(bk0 + th), w,
                                                xT_ap(r, d, slice(th * 512, (th + 1) * 512)),
                                                start=s, stop=e),
                                       reads=[wname, f"xT{r}"], writes=[PSR[bk0 + th]], tag="kqproj")
                        for th in range(2):
                            sap, sreg = stage_slot()
                            P.emit("act", lambda o, sap=sap, bank=bk0 + th, sc=scale:
                                   o.activation(sap, ps(bank), AF.Copy, scale=sc),
                                   reads=[PSR[bk0 + th]], writes=[sreg], tag="kqevac")
                            P.emit("pool", lambda o, sap=sap, h=h, t1=t0 + th * 512, dst=dest:
                                   o.dma_start(dst[h, :, t1:t1 + 512], sap),
                                   reads=[sreg], writes=[dreg], kind="d", tag="kqst")
                for sub in range(8):
                    for vch in range(2):
                        bank = 4 + ((sub * 2 + vch) % 2)
                        tsl = slice(sub * 128, (sub + 1) * 128)
                        for d in range(16):
                            P.emit("pe", lambda o, r=r, d=d, tsl=tsl, vch=vch, bank=bank, s=(d == 0), e=(d == 15):
                                   o.matmul(ps(bank), xT_ap(r, d, tsl),
                                            bf(wv)[:, d * 1024 + vch * 512: d * 1024 + (vch + 1) * 512],
                                            start=s, stop=e),
                                   reads=["wv", f"xT{r}"], writes=[PSR[bank]], tag="vproj")
                        sap, sreg = stage_slot()
                        P.emit("act", lambda o, sap=sap, bank=bank:
                               o.activation(sap, ps(bank), AF.Copy),
                               reads=[PSR[bank]], writes=[sreg], tag="vevac")
                        P.emit("pool", lambda o, sap=sap, t0=t0, sub=sub, vch=vch:
                               o.dma_start(v_h[t0 + sub * 128: t0 + (sub + 1) * 128,
                                               vch * 512:(vch + 1) * 512], sap),
                               reads=[sreg], writes=["d_v"], kind="d", tag="vst")
        A.release()

        for t in ("qT", "kT", "v"):
            if t in taps:
                src = {"qT": qT_h, "kT": kT_h, "v": v_h}[t]
                dreg = {"qT": "d_q", "kT": "d_k", "v": "d_v"}[t]
                P.emit("sp", lambda o, s=src, d2=tap_ext[t]: o.dma_start(d2, s),
                       reads=[dreg], kind="d", tag="tap")

        A.mark()
        ctx_r = A.alloc("ctx", HPC * 4096)
        A.mark()
        maskc_r = A.alloc("maskc", 2048)
        P.emit("sp", lambda o: o.dma_start(bf(maskc_r), maskc_in),
               writes=["maskc"], kind="d", tag="ld_mask")
        kb_ = [A.alloc(f"kb{r}", 4096) for r in range(2)]
        vb_ = [A.alloc(f"vb{r}", 4096) for r in range(2)]
        qb_ = [A.alloc(f"qb{r}", 4096) for r in range(2)]
        PT_ = [A.alloc(f"PT{r}", 32 * 512) for r in range(2)]
        rep_ = [A.alloc(f"rep{r}", 2 * 512) for r in range(2)]

        if PH >= 1:
            for h in range(HPC):
                r = h % 2
                P.emit("sp", lambda o, h=h, r=r: o.dma_start(bf(kb_[r]), kT_h[h]),
                       reads=["d_k"], writes=[f"kb{r}"], kind="d", tag="kld")
                P.emit("sp", lambda o, h=h, r=r: o.dma_start(bf(qb_[r]), qT_h[h]),
                       reads=["d_q"], writes=[f"qb{r}"], kind="d", tag="qld")
                vsrc = v_h[:, h * 128:(h + 1) * 128].rearrange("(t p) n -> p t n", p=128)
                vdst = bf(vb_[r]).rearrange("p (t n) -> p t n", n=128)
                for q4 in range(4):
                    tsl = slice(q4 * 8, q4 * 8 + 8)
                    P.emit("sp", lambda o, s=vsrc[:, tsl, :], d2=vdst[:, tsl, :]: o.dma_start(d2, s),
                           reads=["d_v"], writes=[f"vb{r}"], kind="d", tag="vld")

                pend = [None]

                for s_ in range(NST):
                    NB = s_ + 1
                    Q0 = s_ * STW
                    cxb = 4 + (s_ % 2)
                    dnb = 2
                    sp_ = s_ % 2
                    NKB = 4 * NB
                    ptr = f"PT{sp_}"
                    PT_r = PT_[sp_]

                    for kb in range(NKB):
                        bank = [0, 6][(kb // 2) % 2] + (kb % 2)
                        P.emit("pe", lambda o, kb=kb, bank=bank, r=r, Q0=Q0:
                               o.matmul(ps(bank),
                                        bf(kb_[r])[:, kb * 128:(kb + 1) * 128],
                                        bf(qb_[r])[:, Q0:Q0 + 512],
                                        start=True, stop=True),
                               reads=[f"kb{r}", f"qb{r}"], writes=[PSR[bank]], tag="sc")
                        if kb % 2 == 1:
                            b0 = [0, 6][((kb - 1) // 2) % 2]
                            P.emit("act", lambda o, b0=b0, kb=kb, ptr=ptr, PT_r=PT_r:
                                   o.activation(bf(PT_r)[:, (kb - 1) * 512:(kb + 1) * 512],
                                                ps(b0, 2), AF.Exp, bias=cbias[:, 0:1]),
                                   reads=[PSR[b0], PSR[b0 + 1], "cbias"],
                                   writes=[ptr], tag="exp2")
                            for kbd in (kb - 1, kb):
                                m = kbd - 4 * s_
                                if 0 <= m < 4:
                                    P.emit("dve", lambda o, kbd=kbd, m=m, ptr=ptr, PT_r=PT_r:
                                           o.tensor_mul(bf(PT_r)[:, kbd * 512:(kbd + 1) * 512],
                                                        bf(PT_r)[:, kbd * 512:(kbd + 1) * 512],
                                                        bf(maskc_r)[:, m * 512:(m + 1) * 512]),
                                           reads=[ptr, "maskc"], writes=[ptr], tag="maskd")
                    if pend[0] is not None:
                        pend[0]()
                        pend[0] = None
                    for kb in range(NKB):
                        P.emit("pe", lambda o, kb=kb, PT_r=PT_r, dnb=dnb, s4=(kb == 0), e4=(kb == NKB - 1):
                               o.matmul(ps(dnb)[0:1, :], onec[:, :],
                                        bf(PT_r)[:, kb * 512:(kb + 1) * 512],
                                        start=s4, stop=e4),
                               reads=["onec", ptr], writes=[PSR[dnb]], tag="ones")
                    P.emit("dve", lambda o, dnb=dnb: o.reciprocal(recrow[:, :], ps(dnb)[0:1, :]),
                           reads=[PSR[dnb]], writes=["recrow"], tag="drec")
                    for kb in range(NKB):
                        P.emit("pe", lambda o, kb=kb, r=r, PT_r=PT_r, s4=(kb == 0), e4=(kb == NKB - 1), bank=cxb:
                               o.matmul(ps(bank),
                                        bf(vb_[r])[:, kb * 128:(kb + 1) * 128],
                                        bf(PT_r)[:, kb * 512:(kb + 1) * 512],
                                        start=s4, stop=e4),
                               reads=[f"vb{r}", ptr], writes=[PSR[cxb]], tag="av")

                    def mk_pend(h=h, Q0=Q0, sp_=sp_, cxb=cxb, dnb=dnb):
                        def go():
                            P.emit("pe", lambda o: o.matmul(ps(3), oner[:, :], recrow[:, :],
                                                            start=True, stop=True),
                                   reads=["oner", "recrow"], writes=[PSR[3]], tag="rep")
                            P.emit("act", lambda o: o.activation(f32(rep_[sp_]), ps(3), AF.Copy),
                                   reads=[PSR[3]], writes=[f"rep{sp_}"], tag="repev")
                            P.emit("dve", lambda o:
                                   o.tensor_mul(bf(ctx_r)[:, h * 4096 + Q0: h * 4096 + Q0 + STW],
                                                ps(cxb), f32(rep_[sp_])),
                                   reads=[PSR[cxb], f"rep{sp_}"], writes=[f"ctx{h}"], tag="ctxev")
                        return go
                    pend[0] = mk_pend()
                    pend[0]()
                    pend[0] = None
                if pend[0] is not None:
                    pend[0]()
                    pend[0] = None

                if PH >= 2 and h % 2 == 1:
                    k = h // 2
                    for hh in (h - 1, h):
                        P.emit("sp", lambda o, hh=hh: o.dma_start(
                            ccin.ap()[hh * 128:(hh + 1) * 128, :],
                            bf(ctx_r)[:, hh * 4096:(hh + 1) * 4096].bitcast(F32)),
                            reads=[f"ctx{hh}"], writes=[f"d_ccin{k}", f"ctxg{k}"],
                            kind="d", tag="ccst")
                    if k == 3:
                        P.emit("pool", lambda o: o.collective_compute(
                            "AllGather", ALU.bypass,
                            replica_groups=[[0, 1], [2, 3], [4, 5], [6, 7]],
                            ins=[ccin.ap().opt()], outs=[ccout.ap().opt()]),
                            reads=["d_ccin0", "d_ccin1", "d_ccin2", "d_ccin3"],
                            writes=["d_ccout"], kind="cc", tag="ccg")
        A.release()

        if "ctx" in taps:
            for h in range(HPC):
                P.emit("sp", lambda o, h=h: o.dma_start(
                    tap_ext["ctx"][h * 128:(h + 1) * 128, :], bf(ctx_r)[:, h * 4096:(h + 1) * 4096]),
                    reads=[f"ctx{h}"], kind="d", tag="tapctx")

        def ctx2_chunk(c):
            k = (c % 8) // 2
            pos = (c % 8) % 2 + 2 * (c // 8)
            a = ctx_r[0] + 2 * k * 4096 + pos * 2048
            return arena.ap()[:, a:a + 2048]

        if PH >= 2:
            for k in range(4):
                a = ctx_r[0] + 2 * k * 4096
                dstk = arena.ap()[:, a:a + 4 * 2048].bitcast(F32).rearrange(
                    "p (c n) -> p c n", n=1024)
                for rank in range(2):
                    src = bass.AP(ccout, r_cc[k * 2 + rank],
                                  [[2048, 128], [128 * 2048, 2], [1, 1024]])
                    P.emit("pool", lambda o, d=dstk[:, rank * 2:rank * 2 + 2, :], s=src:
                           o.dma_start(d, s),
                           reads=["d_ccout"],
                           writes=[f"ctxg{k}"], kind="d", tag="ccld")
        if "ctx2" in taps:
            for c in range(16):
                P.emit("sp", lambda o, c=c: o.dma_start(
                    tap_ext["ctx2"][c * 128:(c + 1) * 128, :], ctx2_chunk(c)),
                    reads=[f"ctxg{(c % 8) // 2}"], kind="d", tag="tapctx2")

        if PH >= 3:
            sq_r[0] = A.alloc("sqscrA", 2 * 2048)
            ystg = [A.alloc(f"ystg{r}", 512) for r in range(2)]
            wout_r = A.alloc("wout", 16 * 2048)
            xs_ = [A.alloc(f"xs{r}", 2 * 2048) for r in range(2)]
            h1_ = [A.alloc(f"h1{r}", 2 * 2048) for r in range(2)]
            y_ = [A.alloc(f"yb{r}", 2 * 2048) for r in range(2)]
            dstw = bf(wout_r).rearrange("p (c n) -> p c n", n=2048)
            srcw = wout_in.rearrange("(c p) n -> p c n", p=128)
            for cc4 in range(4):
                csl = slice(cc4 * 4, cc4 * 4 + 4)
                P.emit("pool", lambda o, d=dstw[:, csl, :], s=srcw[:, csl, :]: o.dma_start(d, s),
                       writes=["wout"], kind="d", tag="ld_wout")
            for tt in range(16):
                r = tt % 2
                grp = 0 if r == 0 else 4
                banks = [PSR[grp + i] for i in range(4)]
                P.emit("sp", lambda o, tt=tt, r=r: o.dma_start(
                    f32(xs_[r]), xown_in[tt * 128:(tt + 1) * 128, :]),
                    writes=[f"xs{r}"], kind="d", tag="xld")
                for d in range(16):
                    for ct in range(4):
                        P.emit("pe", lambda o, d=d, ct=ct, tt=tt, grp=grp, s=(d == 0), e=(d == 15):
                               o.matmul(ps(grp + ct),
                                        ctx2_chunk(d)[:, tt * 128:(tt + 1) * 128],
                                        bf(wout_r)[:, d * 2048 + ct * 512: d * 2048 + (ct + 1) * 512],
                                        start=s, stop=e),
                               reads=[f"ctxg{(d % 8) // 2}", "wout"], writes=[banks[ct]], tag="oproj")
                P.emit("dve", lambda o, r=r, grp=grp:
                       o.scalar_tensor_tensor(f32(h1_[r]), ps(grp, 4), 1.0, f32(xs_[r]),
                                              ALU.mult, ALU.add, accum_out=stats[:, r * 8:r * 8 + 1]),
                       reads=banks + [f"xs{r}"], writes=[f"h1{r}", f"lnstats{r}"], tag="h1")
                ln_chain(f32(h1_[r]), f"h1{r}", f32(y_[r]), f"yb{r}", par=r)
                P.emit("sp", lambda o, tt=tt, r=r: o.dma_start(
                    y_h[tt * 128:(tt + 1) * 128, :], f32(y_[r])),
                    reads=[f"yb{r}"], writes=["d_y"], kind="d", tag="yst")
                for dq in range(4):
                    bank = grp + dq % 2
                    yr = (tt * 4 + dq) % 2
                    for ii in range(4):
                        d = dq * 4 + ii
                        P.emit("pe", lambda o, d=d, ii=ii, r=r, bank=bank:
                               o.transpose(ps(bank)[:, ii * 128:(ii + 1) * 128],
                                           f32(y_[r])[:, d * 128:(d + 1) * 128],
                                           identf[:, :]),
                               reads=[f"yb{r}", "identf"], writes=[PSR[bank]], tag="ytr")
                    P.emit("dve", lambda o, bank=bank, yr=yr:
                           o.tensor_copy(bf(ystg[yr]), ps(bank)),
                           reads=[PSR[bank]], writes=[f"ystg{yr}"], tag="ytrev")
                    ydst = yT_hbm.rearrange("(c p) n -> p c n", p=128)
                    P.emit("sp", lambda o, dq=dq, tt=tt, yr=yr, ydst=ydst:
                           o.dma_start(ydst[:, dq * 4:(dq + 1) * 4, tt * 128:(tt + 1) * 128],
                                       bf(ystg[yr]).rearrange("p (c n) -> p c n", n=128)),
                           reads=[f"ystg{yr}"], writes=["d_yT"], kind="d", tag="ytst")
        A.release()

        if "y" in taps:
            P.emit("sp", lambda o: o.dma_start(tap_ext["y"], y_h),
                   reads=["d_y"], kind="d", tag="tapy")

        A.mark()
        yT_r = A.alloc("yT", 16 * 2048)
        if PH >= 4:
            gateb_r = A.alloc("gateb", 16 * 32)
            gatef_r = A.alloc("gatef", 2 * 16 * 32)
            ysrc = yT_hbm.rearrange("(c p) n -> p c n", p=128)
            ydst2 = bf(yT_r).rearrange("p (c n) -> p c n", n=2048)
            for q in range(2):
                csl = slice(q * 8, q * 8 + 8)
                P.emit("sp", lambda o, d=ydst2[:, csl, :], s2=ysrc[:, csl, :]: o.dma_start(d, s2),
                       reads=["d_yT"], writes=["yT"], kind="d", tag="yTld")
            P.emit("sp", lambda o: o.dma_start(
                f32(gatef_r).rearrange("p (c n) -> p c n", n=32),
                gate_in.rearrange("(c p) n -> p c n", p=128)),
                writes=["gatef"], kind="d", tag="ld_gatef")
            P.emit("act", lambda o: o.activation(bf(gateb_r), f32(gatef_r), AF.Copy),
                   reads=["gatef"], writes=["gateb"], tag="gatecast")
            for tt in range(16):
                bank = 6 + tt % 2
                for d in range(16):
                    P.emit("pe", lambda o, d=d, tt=tt, bank=bank, s=(d == 0), e=(d == 15):
                           o.matmul(ps(bank)[:, 0:32],
                                    bf(yT_r)[:, d * 2048 + tt * 128: d * 2048 + (tt + 1) * 128],
                                    bf(gateb_r)[:, d * 32:(d + 1) * 32],
                                    start=s, stop=e),
                           reads=["yT", "gateb"], writes=[PSR[bank]], tag="rlog")
                lg = rt[:, 0:32]
                rp = rt[:, 32:64]
                t8 = rt[:, 64:72]
                zap = rt[:, 96:128]
                P.emit("dve", lambda o, bank=bank, lg=lg: o.tensor_copy(lg, ps(bank)[:, 0:32]),
                       reads=[PSR[bank]], writes=["rt"], tag="rcp")

                P.emit("dve", lambda o, lg=lg: o.reduce_max(rt[:, 72:73], lg, axis=AX.X, negate=True),
                       reads=["rt"], writes=["rt"], tag="rmax")
                P.emit("act", lambda o, lg=lg, rp=rp: o.activation(
                    rp, lg, AF.Exp, bias=rt[:, 72:73], accum_out=rt[:, 73:74]),
                    reads=["rt"], writes=["rt"], tag="rexp")
                P.emit("dve", lambda o: o.reciprocal(rt[:, 74:75], rt[:, 73:74]),
                       reads=["rt"], writes=["rt"], tag="rrec")
                P.emit("dve", lambda o, rp=rp: o.tensor_scalar_mul(rp, rp, rt[:, 74:75]),
                       reads=["rt"], writes=["rt"], tag="rnorm")
                P.emit("dve", lambda o, rp=rp, t8=t8: o.max(out=t8, in_=rp),
                       reads=["rt"], writes=["rt"], tag="rtop8")
                P.emit("dve", lambda o: o.memset(rt[:, 68:72], -1e30),
                       reads=["rt"], writes=["rt"], tag="rmemset")
                P.emit("dve", lambda o, rp=rp, t8=t8, zap=zap: o.match_replace(
                    out=zap, in_to_replace=t8, in_values=rp, imm_value=0.0),
                    reads=["rt"], writes=["rt"], tag="rzap")
                P.emit("dve", lambda o, rp=rp, zap=zap: o.tensor_sub(rt[:, 128:160], rp, zap),
                       reads=["rt"], writes=["rt"], tag="rsub")
                P.emit("dve", lambda o: o.reduce_sum(rt[:, 75:76], rt[:, 64:68], axis=AX.X),
                       reads=["rt"], writes=["rt"], tag="rsum4")
                P.emit("dve", lambda o: o.reciprocal(rt[:, 76:77], rt[:, 75:76]),
                       reads=["rt"], writes=["rt"], tag="rrec4")
                P.emit("dve", lambda o, tt=tt: o.tensor_scalar_mul(
                    RW[:, tt * 32:(tt + 1) * 32], rt[:, 128:160], rt[:, 76:77]),
                    reads=["rt"], writes=["RW"], tag="rwst")
        if "rw" in taps:
            P.emit("sp", lambda o: o.dma_start(
                tap_ext["rw"].rearrange("(t p) n -> p t n", p=128),
                RW.ap().rearrange("p (t n) -> p t n", n=32)),
                reads=["RW"], kind="d", tag="taprw")

        if PH >= 5:
            wguA = A.alloc("wguA", 16 * 768)
            wguB = [A.alloc(f"wguB{r}", 16 * 768) for r in range(2)]
            sg_r = A.alloc("sg", 6 * 2048)
            gt_r = A.alloc("gt", 6 * 2048)

            def wgu_load(e, half):
                srcw2 = gup_in[e].rearrange("(c p) n -> p c n", p=128)
                reg, rname = (wguA, "wguA") if half == 0 else (wguB[e % 2], f"wguB{e % 2}")
                dstw2 = bf(reg).rearrange("p (c n) -> p c n", n=768)
                for q in range(2):
                    csl = slice(q * 8, q * 8 + 8)
                    P.emit("pool", lambda o, d=dstw2[:, csl, :],
                           s=srcw2[:, csl, half * 768:(half + 1) * 768]:
                           o.dma_start(d, s),
                           writes=[rname], kind="d", tag="ld_wgu")
            wgu_load(0, 0)
            wgu_load(0, 1)
            wgu_load(1, 1)
            for e in range(NE):
                for gci, gc in enumerate(list(range(6)) + list(range(6, 12))):
                    bank0 = (gc % 2) * 4
                    wreg, wr = (wguA, "wguA") if gc < 6 else (wguB[e % 2], f"wguB{e % 2}")
                    gl_ = gc if gc < 6 else gc - 6
                    for d in range(16):
                        for mt in range(4):
                            P.emit("pe", lambda o, gl_=gl_, d=d, mt=mt, wreg=wreg, bank0=bank0,
                                   s=(d == 0), e2=(d == 15):
                                   o.matmul(ps(bank0 + mt),
                                            bf(wreg)[:, d * 768 + gl_ * 128: d * 768 + (gl_ + 1) * 128],
                                            bf(yT_r)[:, d * 2048 + mt * 512: d * 2048 + (mt + 1) * 512],
                                            start=s, stop=e2),
                                   reads=[wr, "yT"], writes=[PSR[bank0 + mt]], tag="gu")
                    if gci == 5 and e + 1 < NE:
                        wgu_load(e + 1, 0)
                    if gci == 11 and e + 2 < NE:
                        wgu_load(e + 2, 1)
                    if gc < 6:
                        for mt in range(4):
                            P.emit("act", lambda o, gc=gc, mt=mt, bank0=bank0:
                                   o.activation(bf(sg_r)[:, gc * 2048 + mt * 512: gc * 2048 + (mt + 1) * 512],
                                                ps(bank0 + mt), AF.Silu),
                                   reads=[PSR[bank0 + mt]], writes=["sg"], tag="silu")
                    else:
                        i = gc - 6
                        for mt in range(4):
                            P.emit("dve", lambda o, i=i, mt=mt, bank0=bank0:
                                   o.tensor_mul(bf(gt_r)[:, i * 2048 + mt * 512: i * 2048 + (mt + 1) * 512],
                                                bf(sg_r)[:, i * 2048 + mt * 512: i * 2048 + (mt + 1) * 512],
                                                ps(bank0 + mt)),
                                   reads=[PSR[bank0 + mt], "sg"],
                                   writes=[f"gt{i}"], tag="gmul")
                        P.emit("sp", lambda o, e=e, i=i:
                               o.dma_start(gated_h[e][i * 128:(i + 1) * 128, :],
                                           bf(gt_r)[:, i * 2048:(i + 1) * 2048]),
                               reads=[f"gt{i}"], writes=[f"d_g{e}"], kind="d", tag="gst")
        A.release()
        if "gated" in taps:
            for e in range(2):
                P.emit("sp", lambda o, e=e: o.dma_start(tap_ext["gated"][e], gated_h[e]),
                       reads=[f"d_g{e}"], kind="d", tag="tapg")

        A.mark()
        acc_r = A.alloc("acc", 16 * 2 * 2048)

        def acc_ap(tt, csl=slice(0, 2048)):
            a = acc_r[0]
            return arena.ap()[:, a + tt * 4096: a + (tt + 1) * 4096].bitcast(F32)[:, csl]

        if PH >= 6:
            A.mark()
            gl_ = [A.alloc(f"gl{r}", 3 * 2048) for r in range(2)]
            wd_ = [A.alloc(f"wd{r}", 6 * 2048) for r in range(2)]
            for tt in range(16):
                P.emit("dve", lambda o, tt=tt: o.memset(acc_ap(tt), 0.0),
                       writes=[f"acc{tt}"], tag="accz")

            def wd_load(e):
                slot = e % 2
                wsrc = gdn_in[e].rearrange("(c p) n -> p c n", p=128)
                for half in range(2):
                    P.emit("pool", lambda o, slot=slot, s=wsrc[:, half * 3:half * 3 + 3, :],
                           half=half:
                           o.dma_start(bf(wd_[slot]).rearrange("p (c n) -> p c n", n=2048)[:, half * 3:half * 3 + 3, :], s),
                           writes=[f"wd{slot}"], kind="d", tag="ld_wd")
            wd_load(0)
            for e in range(NE):
                for half in range(2):
                    gsrc = gated_h[e].rearrange("(c p) n -> p c n", p=128)
                    P.emit("sp", lambda o, slot=half, s=gsrc[:, half * 3:half * 3 + 3, :]:
                           o.dma_start(bf(gl_[slot]).rearrange("p (c n) -> p c n", n=2048), s),
                           reads=[f"d_g{e}"], writes=[f"gl{half}"], kind="d", tag="gld")
                if e + 1 < NE:
                    wd_load(e + 1)
                for tt in range(16):
                    grp = (tt % 2) * 4
                    for i in range(6):
                        slot = i // 3
                        il = i % 3
                        for ct in range(4):
                            P.emit("pe", lambda o, i=i, il=il, ct=ct, tt=tt, slot=slot, grp=grp,
                                   ws=e % 2, s=(i == 0), e2=(i == 5):
                                   o.matmul(ps(grp + ct),
                                            bf(gl_[slot])[:, il * 2048 + tt * 128: il * 2048 + (tt + 1) * 128],
                                            bf(wd_[ws])[:, i * 2048 + ct * 512: i * 2048 + (ct + 1) * 512],
                                            start=s, stop=e2),
                                   reads=[f"gl{slot}", f"wd{e % 2}"], writes=[PSR[grp + ct]], tag="dn")
                    for ct in range(4):
                        P.emit("dve", lambda o, tt=tt, ct=ct, e=e, grp=grp:
                               o.scalar_tensor_tensor(
                                   acc_ap(tt, slice(ct * 512, (ct + 1) * 512)),
                                   ps(grp + ct), RW[:, tt * 32 + e: tt * 32 + e + 1],
                                   acc_ap(tt, slice(ct * 512, (ct + 1) * 512)),
                                   ALU.mult, ALU.add),
                               reads=[PSR[grp + ct], "RW", f"acc{tt}"], writes=[f"acc{tt}"], tag="dacc")
            A.release()
        if "acc" in taps:
            for tt in range(16):
                P.emit("sp", lambda o, tt=tt: o.dma_start(
                    tap_ext["acc"][tt * 128:(tt + 1) * 128, :], acc_ap(tt)),
                    reads=[f"acc{tt}"], kind="d", tag="tapacc")

        if PH >= 7:
            sq_r[0] = A.alloc("sqscrB", 2 * 2048)
            yl_ = [A.alloc(f"yl{r}", 2 * 2048) for r in range(2)]
            h2_ = [A.alloc(f"h2{r}", 2 * 2048) for r in range(2)]
            ob_ = [A.alloc(f"ob{r}", 2 * 2048) for r in range(2)]
            for tt in range(16):
                r = tt % 2
                P.emit("sp", lambda o, tt=tt, r=r: o.dma_start(
                    f32(yl_[r]), y_h[tt * 128:(tt + 1) * 128, :]),
                    reads=["d_y"], writes=[f"yl{r}"], kind="d", tag="yld")
                P.emit("dve", lambda o, tt=tt, r=r:
                       o.scalar_tensor_tensor(f32(h2_[r]), acc_ap(tt), 1.0,
                                              f32(yl_[r]), ALU.mult, ALU.add,
                                              accum_out=stats[:, r * 8:r * 8 + 1]),
                       reads=[f"acc{tt}", f"yl{r}"], writes=[f"h2{r}", f"lnstats{r}"], tag="h2")
                ln_chain(f32(h2_[r]), f"h2{r}", f32(ob_[r]), f"ob{r}", par=r)
                P.emit("sp", lambda o, tt=tt, r=r: o.dma_start(
                    out_ext[tt * 128:(tt + 1) * 128, :], f32(ob_[r])),
                    reads=[f"ob{r}"], writes=["d_out"], kind="d", tag="outst")
        A.release()

        totals = P.finalize()

        def replay(e):
            def go(engobj):
                for op in P.streams[e]:
                    for (snm, val) in op.waits:
                        engobj.wait_ge(sems[snm], val)
                    ins = op.fn(engobj)
                    if op.needs_inc and ins is not None:
                        ins.then_inc(sems[op.inc_sem], 16 if op.kind == "d" else 1)
                if e == "sp":
                    engobj.wait_ge(sems["sd"], totals["sd"])
                    if totals["pd"]:
                        engobj.wait_ge(sems["pd"], totals["pd"])
            return go

        block.tensor(replay("pe"))
        block.vector(replay("dve"))
        block.scalar(replay("act"))
        block.gpsimd(replay("pool"))
        block.sync(replay("sp"))

    return nc


_CACHE = {}


def make_in_maps(L0_attn_x, L0_attn_w_qkv, L0_attn_w_out, L0_ffn_gate,
                 L0_ffn_gate_up_proj, L0_ffn_down_proj):
    x = np.ascontiguousarray(np.asarray(L0_attn_x, dtype=np.float32))
    wqkv = np.asarray(L0_attn_w_qkv, dtype=np.float32)
    wout = np.ascontiguousarray(np.asarray(L0_attn_w_out, dtype=np.float32))
    gate = np.ascontiguousarray(np.asarray(L0_ffn_gate, dtype=np.float32))
    gup = np.ascontiguousarray(np.asarray(L0_ffn_gate_up_proj, dtype=np.float32))
    gdn = np.ascontiguousarray(np.asarray(L0_ffn_down_proj, dtype=np.float32))
    ident, identf, maskc = _host_consts()
    in_maps = []
    for c in range(8):
        b, p = c // 2, c % 2
        wq = wqkv[:, p * 1024:(p + 1) * 1024]
        wk = wqkv[:, 2048 + p * 1024: 2048 + (p + 1) * 1024]
        wv = wqkv[:, 4096 + p * 1024: 4096 + (p + 1) * 1024]
        in_maps.append({
            "x": np.ascontiguousarray(x[b]),
            "xown": np.ascontiguousarray(x[b, p * 2048:(p + 1) * 2048]),
            "wqkv": np.ascontiguousarray(np.concatenate([wq, wk, wv], axis=1)),
            "wout": wout,
            "gate": gate,
            "gup": gup,
            "gdn": gdn,
            "ident": ident,
            "identf": identf,
            "maskc": maskc,
        })
    return in_maps


def kernel(L0_attn_x, L0_attn_w_qkv, L0_attn_w_out, L0_ffn_gate,
           L0_ffn_gate_up_proj, L0_ffn_down_proj):
    if "nc" not in _CACHE:
        _CACHE["nc"] = build_program()
    nc = _CACHE["nc"]
    in_maps = make_in_maps(L0_attn_x, L0_attn_w_qkv, L0_attn_w_out, L0_ffn_gate,
                           L0_ffn_gate_up_proj, L0_ffn_down_proj)
    res = run_bass_kernel_spmd(nc, in_maps, core_ids=list(range(8)))
    out = np.empty((B, S, D), dtype=np.float32)
    for c in range(8):
        b, p = c // 2, c % 2
        out[b, p * 2048:(p + 1) * 2048] = res.results[c]["out"]
    return out
